# version 24
# baseline (speedup 1.0000x reference)
import os
import numpy as np
import ml_dtypes
import concourse.bass as bass
import concourse.mybir as mybir
from concourse.bass_utils import run_bass_kernel_spmd

F32 = mybir.dt.float32
BF16 = mybir.dt.bfloat16
AF = mybir.ActivationFunctionType
ALU = mybir.AluOpType
AX = mybir.AxisListType.X

N_CORES = 8
B_PER_CORE = 2
N_TOK = 4096
D = 1024
NCH = 8
N_TILES = 32
N_STRIPES = 8
N_PAIRS = 4
IB = 1024
N_IB = 4
N_JC = 32
QB = 128.0
CLIP_HI = float(np.float32(128.0) - np.float32(1e-5))
CLIP_LO = float(np.float32(-128.0) + np.float32(1e-5))
STAGE = int(os.environ.get("KSTAGE", "3"))

bf16 = ml_dtypes.bfloat16


class Sem:
    def __init__(self, nc, es, name):
        self.h = es.enter_context(nc.semaphore(name))
        self.n = 0

    def inc(self, instr, k=1):
        instr.then_inc(self.h, k)
        self.n += k
        return instr

    def dma(self, instr):
        instr.then_inc(self.h, 16)
        self.n += 16
        return instr


def mk(lst, j):
    return lst[j] if 0 <= j < len(lst) else 0


def build_program():
    nc = bass.Bass("TRN2", target_bir_lowering=False)

    x_in = nc.dram_tensor("x_in", [B_PER_CORE, N_TOK, D], F32, kind="ExternalInput").ap()
    wq_d = nc.dram_tensor("wq_d", [D, 512], F32, kind="ExternalInput").ap()
    wkv_d = nc.dram_tensor("wkv_d", [D, 1024], F32, kind="ExternalInput").ap()
    wo_d = nc.dram_tensor("wo_d", [512, D], F32, kind="ExternalInput").ap()
    w_d = nc.dram_tensor("w_d", [D, D], F32, kind="ExternalInput").ap()
    lg_d = nc.dram_tensor("lg_d", [D], F32, kind="ExternalInput").ap()
    lb_d = nc.dram_tensor("lb_d", [D], F32, kind="ExternalInput").ap()
    c_bones8 = nc.dram_tensor("c_bones8", [128, 32], BF16, kind="ExternalInput").ap()
    c_sel8 = nc.dram_tensor("c_sel8", [8, 512], BF16, kind="ExternalInput").ap()
    c_ones64 = nc.dram_tensor("c_ones64", [1, 64], BF16, kind="ExternalInput").ap()
    c_ones1f = nc.dram_tensor("c_ones1f", [1, 128], F32, kind="ExternalInput").ap()
    c_ones128f = nc.dram_tensor("c_ones128f", [128, 1], F32, kind="ExternalInput").ap()

    out_d = nc.dram_tensor("out", [B_PER_CORE, N_TOK, D], F32, kind="ExternalOutput").ap()

    d_xnt = nc.dram_tensor("d_xnt", [B_PER_CORE, N_TOK, D], BF16).ap()
    d_xnT = nc.dram_tensor("d_xnT", [B_PER_CORE, D, N_TOK], BF16).ap()
    d_x1 = nc.dram_tensor("d_x1", [B_PER_CORE, N_TOK, D], F32).ap()
    d_x2 = nc.dram_tensor("d_x2", [B_PER_CORE, N_TOK, D], F32).ap()
    d_xln = nc.dram_tensor("d_xln", [B_PER_CORE, N_TOK, D], BF16).ap()
    d_gred = nc.dram_tensor("d_gred", [128, 1], F32).ap()
    cc_in1 = nc.dram_tensor("cc_in1", [1, 1], F32).ap()
    cc_out1 = nc.dram_tensor("cc_out1", [1, 1], F32).ap()
    cc_in3 = nc.dram_tensor("cc_in3", [1, 1], F32).ap()
    cc_out3 = nc.dram_tensor("cc_out3", [1, 1], F32).ap()

    from contextlib import ExitStack
    es = ExitStack()
    S = lambda name, shape, dt: es.enter_context(nc.sbuf_tensor(name, shape, dt))
    with es:
        w_qs = S("w_qs", [128, 8 * 1024], BF16)
        w_a = S("w_a", [128, 4 * 1024], BF16)
        wk_sb = S("wk_sb", [128, 8 * 512], BF16)
        wv_sb = S("wv_sb", [128, 8 * 512], BF16)
        g_sb = S("g_sb", [128, 8], F32)
        b_sb = S("b_sb", [128, 8], F32)
        bones8 = S("bones8", [128, 32], BF16)
        sel8 = S("sel8", [8, 512], BF16)
        ones64 = S("ones64", [1, 64], BF16)
        ones1f = S("ones1f", [1, 128], F32)
        ones128f = S("ones128f", [128, 1], F32)
        c16 = S("c16", [128, 1], F32)
        cm16 = S("cm16", [128, 1], F32)
        sabs = S("sabs", [128, 8], F32)
        tny = S("tny", [1, 8], F32)
        t128 = S("t128", [128, 2], F32)
        qT_sb = S("qT_sb", [128, 4 * 4096], BF16)
        kT_sb = S("kT_sb", [128, 4 * 4096], BF16)
        v_aug = S("v_aug", [128, N_JC * 520], BF16)
        ao_sb0 = S("ao_sb0", [128, 4096], BF16)
        aA0 = S("aA0", [128, 4096], BF16)
        aA1 = S("aA1", [128, 4096], BF16)
        xsrc0 = S("xsrc0", [128, 1024], F32)
        xsrc1 = S("xsrc1", [128, 1024], F32)
        xw0 = S("xw0", [128, 1024], F32)
        xw1 = S("xw1", [128, 1024], F32)
        xnp0 = S("xnp0", [128, 1024], BF16)
        xnp1 = S("xnp1", [128, 1024], BF16)
        bnt = S("bnt", [128, 12], F32)
        mv = S("mv", [128, 2], F32)
        stt = S("sttt", [128, 8], F32)
        chmax = S("chmax", [128, 16], F32)
        lmax = S("lmax", [128, 1], F32)
        grow = S("grow", [1, 128], F32)
        g11 = S("g11", [1, 4], F32)
        scpack = S("scpack", [1, 2], F32)
        sqsy = S("sqsy", [128, 2], F32)
        nrec = S("nrec", [8, 1024], F32)
        rinv_n = S("rinv_n", [8, 1024], BF16)
        rs_ib = S("rs_ib", [1, 2048], F32)
        rinv_ib = S("rinv_ib", [1, 2048], BF16)

        ph0 = aA0[:, 0:2048]
        ph1 = aA0[:, 2048:4096]
        sq0 = kT_sb[:, 0:4096]
        sq1 = kT_sb[:, 4096:8192]
        tmpg = ao_sb0[:, 0:1024]
        tmpl = ao_sb0[:, 1024:2048]
        wst = xw0

        def ao_view(m):
            if m == 0:
                return ao_sb0
            return qT_sb[:, (m - 1) * 4096: m * 4096]

        pA = es.enter_context(nc.psum_tensor("pA", [128, 2048], F32))
        pB = es.enter_context(nc.psum_tensor("pB", [128, 2048], F32))

        def pslot(s):
            s = s % 6
            if s < 4:
                return pA[:, s * 512:(s + 1) * 512]
            return pB[:, (s - 4) * 512:(s - 4 + 1) * 512]

        sm = {}
        for name in ["ld", "gld", "vs", "ss", "ts", "ph",
                     "dA0", "dA1", "dB0", "dB1", "dN0", "dN1",
                     "dL0", "dL1", "dX0", "dX1", "dT0", "dT1",
                     "v2s", "s2v", "eA", "mS", "mPV", "evD",
                     "cg1", "cg2", "cg3", "cg4", "ccc"]:
            sm[name] = Sem(nc, es, name)
        ld, gld, vs, ss, ts, ph = (sm[k] for k in ["ld", "gld", "vs", "ss", "ts", "ph"])
        dA = (sm["dA0"], sm["dA1"])
        dB = (sm["dB0"], sm["dB1"])
        dN = (sm["dN0"], sm["dN1"])
        dL = (sm["dL0"], sm["dL1"])
        dX = (sm["dX0"], sm["dX1"])
        dT = (sm["dT0"], sm["dT1"])
        v2s, s2v, eA, mS, mPV, evD = (sm[k] for k in
                                      ["v2s", "s2v", "eA", "mS", "mPV", "evD"])
        cg1, cg2, cg3, cg4, ccc = (sm[k] for k in ["cg1", "cg2", "cg3", "cg4", "ccc"])

        blk = es.enter_context(nc.Block())
        ENGS = ["sync", "vector", "scalar", "tensor", "gpsimd"]
        wo_done_vs = [0]

        def fence():
            for e in ENGS:
                def _f(eng):
                    eng.nop().then_inc(ph.h, 1)
                getattr(blk, e)(_f)
            ph.n += len(ENGS)
            tgt = ph.n
            for e in ENGS:
                def _g(eng, tgt=tgt):
                    eng.wait_ge(ph.h, tgt)
                getattr(blk, e)(_g)

        def init_gate():
            vals = [(ld, ld.n), (gld, gld.n), (vs, vs.n), (ts, ts.n)]
            for e in ENGS:
                def _g(eng, vals=vals):
                    for s, v in vals:
                        if v:
                            eng.wait_ge(s.h, v)
                getattr(blk, e)(_g)

        def wait_stream_totals(eng, streams):
            for s2 in streams:
                for s in s2:
                    if s.n:
                        eng.wait_ge(s.h, s.n)

        def emit_init():
            def _sync(sync):
                sync.dma_start(bones8[:], c_bones8).then_inc(ld.h, 16)
                sync.dma_start(sel8[:], c_sel8).then_inc(ld.h, 16)
                sync.dma_start(ones64[:], c_ones64).then_inc(ld.h, 16)
                sync.dma_start(ones1f[:], c_ones1f).then_inc(ld.h, 16)
                sync.dma_start(ones128f[:], c_ones128f).then_inc(ld.h, 16)
                sync.dma_start(g_sb[:], lg_d.rearrange("(k p) -> p k", p=128),
                               allow_slow_non_contiguous=True).then_inc(ld.h, 16)
                sync.dma_start(b_sb[:], lb_d.rearrange("(k p) -> p k", p=128),
                               allow_slow_non_contiguous=True).then_inc(ld.h, 16)
                ld.n += 7 * 16
                for cch in range(8):
                    if cch >= 1:
                        sync.wait_ge(vs.h, cch)
                    sync.dma_start(wst[:], w_d[cch * 128:(cch + 1) * 128, :]).then_inc(ld.h, 16)
                    ld.n += 16
            blk.sync(_sync)

            def _gp(gp):
                gp.dma_start(w_a[:].rearrange("p (k j) -> p k j", j=512),
                             wq_d.rearrange("(k p) j -> p k j", p=128)).then_inc(gld.h, 16)
                gp.dma_start(wk_sb[:].rearrange("p (k j) -> p k j", j=512),
                             wkv_d[:, 0:512].rearrange("(k p) j -> p k j", p=128)).then_inc(gld.h, 16)
                gp.dma_start(wv_sb[:].rearrange("p (k j) -> p k j", j=512),
                             wkv_d[:, 512:1024].rearrange("(k p) j -> p k j", p=128)).then_inc(gld.h, 16)
                gld.n += 3 * 16
                gp.wait_ge(gld.h, gld.n)
            blk.gpsimd(_gp)

            def _v1(vector):
                vector.memset(c16[:], 16.0)
                vector.memset(cm16[:], -16.0)
                for cch in range(8):
                    vector.wait_ge(ld.h, (7 + cch + 1) * 16)
                    vs.inc(vector.tensor_reduce(sabs[:, cch:cch + 1], wst[:],
                                                axis=AX, op=ALU.add, apply_absolute_value=True))
            blk.vector(_v1)

            def _t1(tensor):
                tensor.wait_ge(ld.h, ld.n)
                tensor.wait_ge(vs.h, vs.n)
                ts.inc(tensor.matmul(pB[0:1, 0:8], ones128f[:, 0:1], sabs[:],
                                     start=True, stop=True))
            blk.tensor(_t1)

            def _v2(vector):
                vector.wait_ge(ts.h, ts.n)
                vs.inc(vector.tensor_reduce(tny[0:1, 0:1], pB[0:1, 0:8], axis=AX, op=ALU.add))
                vector.wait_ge(vs.h, vs.n)
                vs.inc(vector.tensor_scalar(tny[0:1, 1:2], tny[0:1, 0:1],
                                            1.0 / float(D * D), None, ALU.mult))
                vector.wait_ge(vs.h, vs.n)
                vector.tensor_scalar(tny[0:1, 2:3], tny[0:1, 1:2], 1e-5, 0.5, ALU.add, ALU.mult)
                vs.inc(vector.tensor_scalar(tny[0:1, 4:5], tny[0:1, 1:2], 0.5 / QB, None, ALU.mult))
                vector.wait_ge(vs.h, vs.n)
                vs.inc(vector.tensor_scalar(tny[0:1, 3:4], tny[0:1, 2:3], -1.0, None, ALU.mult))
            blk.vector(_v2)

            def _t2(tensor):
                tensor.wait_ge(vs.h, vs.n)
                ts.inc(tensor.matmul(pB[:, 512:514], ones1f[0:1, :], tny[0:1, 2:4],
                                     start=True, stop=True))
            blk.tensor(_t2)

            def _v3(vector):
                vector.wait_ge(ts.h, ts.n)
                vs.inc(vector.tensor_copy(t128[:], pB[:, 512:514]))
            blk.vector(_v3)

            vbase = [0]

            def _sync2(sync):
                sync.wait_ge(vs.h, vs.n)
                vbase[0] = vs.n
                for cch in range(8):
                    if cch >= 1:
                        sync.wait_ge(vs.h, vbase[0] + 2 * cch)
                    sync.dma_start(wst[:], w_d[cch * 128:(cch + 1) * 128, :]).then_inc(ld.h, 16)
                    ld.n += 16
            blk.sync(_sync2)

            def _v4(vector):
                base = ld.n - 8 * 16
                vector.wait_ge(vs.h, vs.n)
                for cch in range(8):
                    vector.wait_ge(ld.h, base + (cch + 1) * 16)
                    vector.tensor_scalar(tmpg, wst[:], t128[:, 0:1], None, ALU.is_gt)
                    vs.inc(vector.tensor_scalar(tmpl, wst[:], t128[:, 1:2], None, ALU.is_lt))
                    vector.wait_ge(vs.h, vs.n)
                    vs.inc(vector.tensor_tensor(w_qs[:, cch * 1024:(cch + 1) * 1024],
                                                tmpg, tmpl, ALU.subtract))
            blk.vector(_v4)

        def ln_vec1(vector, src_ap, mode):
            vector.wait_ge(vs.h, vs.n)
            vector.bn_stats(bnt[:, 0:6], src_ap[:, 0:512])
            vs.inc(vector.bn_stats(bnt[:, 6:12], src_ap[:, 512:1024]))
            vector.wait_ge(vs.h, vs.n)
            vs.inc(vector.bn_aggr(mv[:], bnt[:]))
            vector.wait_ge(vs.h, vs.n)
            vector.tensor_scalar(stt[:, 0:1], mv[:, 0:1], -1.0, None, ALU.mult)
            if mode == "ffn":
                vs.inc(vector.tensor_scalar(stt[:, 1:2], mv[:, 1:2], 1.001, 1e-8, ALU.mult, ALU.add))
            else:
                vs.inc(vector.tensor_scalar(stt[:, 1:2], mv[:, 1:2], 1e-5, None, ALU.add))
            vector.wait_ge(vs.h, vs.n)
            v2s.inc(vector.reciprocal(stt[:, 2:3], stt[:, 1:2]))

        def ln_scalar(scalar):
            scalar.wait_ge(v2s.h, v2s.n)
            s2v.inc(scalar.activation(stt[:, 3:4], stt[:, 2:3], AF.Sqrt))

        def ln_vec2(vector, src_ap, xnp_ap):
            vector.wait_ge(s2v.h, s2v.n)
            vs.inc(vector.tensor_scalar(xnp_ap, src_ap, stt[:, 0:1], stt[:, 3:4],
                                        ALU.add, ALU.mult))

        def emit_ffn_A(b, src_dram, mode):
            aff = []
            for t in range(N_TILES):
                p = t % 2
                buf = [xsrc0, xsrc1][p]
                xnp = [xnp0, xnp1][p]

                def _sy(sync, t=t, p=p, buf=buf):
                    if t >= 2:
                        sync.wait_ge(vs.h, mk(aff, t - 2))
                    dA[p].dma(sync.dma_start(buf[:], src_dram[b, t * 128:(t + 1) * 128, :]))
                blk.sync(_sy)

                def _v1(vector, p=p, buf=buf):
                    vector.wait_ge(dA[p].h, dA[p].n)
                    ln_vec1(vector, buf[:], mode)
                blk.vector(_v1)

                blk.scalar(ln_scalar)

                def _v2(vector, t=t, p=p, buf=buf, xnp=xnp):
                    vector.wait_ge(dN[p].h, dN[p].n)
                    ln_vec2(vector, buf[:], xnp[:])
                    aff.append(vs.n)
                blk.vector(_v2)

                def _gp(gp, t=t, p=p, xnp=xnp):
                    gp.wait_ge(vs.h, aff[-1])
                    dN[p].dma(gp.dma_start(d_xnt[b, t * 128:(t + 1) * 128, :], xnp[:]))
                blk.gpsimd(_gp)

        def emit_ffn_B1(b, cmcol0):
            done = []

            def _pre(sync):
                wait_stream_totals(sync, [dN])
                if sm["mPV"].n:
                    sync.wait_ge(sm["mPV"].h, sm["mPV"].n)
            blk.sync(_pre)

            for k in range(NCH):
                p = k % 2
                buf = [aA0, aA1][p]

                def _sy(sync, k=k, p=p, buf=buf):
                    sync.wait_ge(dT[p].h, dT[p].n)
                    dB[p].dma(sync.dma_start(buf[:], d_xnt[b, :, k * 128:(k + 1) * 128],
                                             transpose=True))
                blk.sync(_sy)

                def _v(vector, k=k, p=p, buf=buf):
                    vector.wait_ge(dB[p].h, dB[p].n)
                    vs.inc(vector.tensor_scalar(buf[:], buf[:], g_sb[:, k:k + 1],
                                                b_sb[:, k:k + 1], ALU.mult, ALU.add))
                    vector.wait_ge(vs.h, vs.n)
                    vs.inc(vector.tensor_reduce(chmax[:, cmcol0 + k:cmcol0 + k + 1], buf[:],
                                                axis=AX, op=ALU.max, apply_absolute_value=True))
                    done.append(vs.n)
                blk.vector(_v)

                def _gp(gp, k=k, p=p, buf=buf):
                    gp.wait_ge(vs.h, done[-1])
                    dT[p].dma(gp.dma_start(d_xnT[b, k * 128:(k + 1) * 128, :], buf[:]))
                blk.gpsimd(_gp)

        def emit_gamma(ccin, ccout):
            def _v1(vector):
                vector.wait_ge(vs.h, vs.n)
                vs.inc(vector.tensor_reduce(lmax[:], chmax[:], axis=AX, op=ALU.max))
            blk.vector(_v1)

            def _gp1(gp):
                gp.wait_ge(vs.h, vs.n)
                cg1.dma(gp.dma_start(d_gred, lmax[:]))
            blk.gpsimd(_gp1)

            def _sy(sync):
                sync.wait_ge(cg1.h, cg1.n)
                cg2.dma(sync.dma_start(grow[:], d_gred.rearrange("p one -> one p")))
            blk.sync(_sy)

            def _v2(vector):
                vector.wait_ge(cg2.h, cg2.n)
                vs.inc(vector.tensor_reduce(g11[0:1, 0:1], grow[:], axis=AX, op=ALU.max))
            blk.vector(_v2)

            def _sc1(scalar):
                scalar.wait_ge(vs.h, vs.n)
                cg3.dma(scalar.dma_start(ccin, g11[0:1, 0:1]))
            blk.scalar(_sc1)

            def _gp2(gp):
                gp.wait_ge(cg3.h, cg3.n)
                ccc.inc(gp.collective_compute(
                    "AllReduce", ALU.max, replica_groups=[list(range(N_CORES))],
                    ins=[ccin], outs=[ccout]), 1)
            blk.gpsimd(_gp2)

            def _sc2(scalar):
                scalar.wait_ge(ccc.h, ccc.n)
                cg4.dma(scalar.dma_start(g11[0:1, 1:2], ccout))
            blk.scalar(_sc2)

            def _v3(vector):
                vector.wait_ge(cg4.h, cg4.n)
                vs.inc(vector.reciprocal(g11[0:1, 2:3], g11[0:1, 1:2]))
                vector.wait_ge(vs.h, vs.n)
                vector.tensor_scalar(scpack[0:1, 0:1], g11[0:1, 2:3], QB, None, ALU.mult)
                vs.inc(vector.tensor_scalar(scpack[0:1, 1:2], g11[0:1, 1:2], tny[0:1, 4:5],
                                            None, ALU.mult))
            blk.vector(_v3)

            def _t(tensor):
                tensor.wait_ge(vs.h, vs.n)
                ts.inc(tensor.matmul(pB[:, 1024:1026], ones1f[0:1, :], scpack[0:1, :],
                                     start=True, stop=True))
            blk.tensor(_t)

            def _v4(vector):
                vector.wait_ge(ts.h, ts.n)
                vs.inc(vector.tensor_copy(sqsy[:], pB[:, 1024:1026]))
            blk.vector(_v4)

        def emit_ffn_B2(b, xres_dram, xdst_dram, ln_mode, res_is_x2):
            qm = []
            mmm = []
            smm = []
            rm = []

            def _pre(sync):
                wait_stream_totals(sync, [dT] + ([dX] if res_is_x2 else []))
                if sm["mPV"].n:
                    sync.wait_ge(sm["mPV"].h, sm["mPV"].n)
            blk.sync(_pre)

            for I in range(N_STRIPES):
                p = I % 2
                sa = [aA0, aA1][p]
                sq = [sq0, sq1][p]

                def _sy1(sync, I=I, p=p, sa=sa):
                    if I >= 2:
                        sync.wait_ge(vs.h, mk(qm, I - 2))
                    for cch in range(NCH):
                        dB[p].dma(sync.dma_start(
                            sa[:, cch * 512:(cch + 1) * 512],
                            d_xnT[b, cch * 128:(cch + 1) * 128, I * 512:(I + 1) * 512]))
                blk.sync(_sy1)

                def _vq(vector, I=I, p=p, sa=sa, sq=sq):
                    vector.wait_ge(vs.h, vs.n)
                    vector.wait_ge(dB[p].h, dB[p].n)
                    if I >= 2:
                        vector.wait_ge(ts.h, mk(smm, I - 2))
                    vs.inc(vector.tensor_scalar(sq, sa[:], sqsy[:, 0:1], CLIP_HI,
                                                ALU.mult, ALU.min))
                    vector.wait_ge(vs.h, vs.n)
                    vs.inc(vector.tensor_scalar(sq, sq, CLIP_LO, None, ALU.max))
                    qm.append(vs.n)
                blk.vector(_vq)

                for it in range(4):
                    t = I * 4 + it
                    tp = t % 2
                    buf = [xsrc0, xsrc1][tp]
                    xw = [xw0, xw1][tp]
                    ybank = tp * 1024

                    def _sy2(sync, t=t, tp=tp, buf=buf):
                        if t >= 2:
                            sync.wait_ge(vs.h, mk(rm, t - 2))
                        dA[tp].dma(sync.dma_start(buf[:],
                                                  xres_dram[b, t * 128:(t + 1) * 128, :]))
                    blk.sync(_sy2)

                    def _t(tensor, t=t, sq=sq, it=it, ybank=ybank, ean=eA.n):
                        if t == 0 and ean:
                            tensor.wait_ge(eA.h, ean)
                        tensor.wait_ge(vs.h, qm[-1])
                        if t >= 2:
                            tensor.wait_ge(vs.h, mk(rm, t - 2))
                        for cch in range(NCH):
                            lhs = sq[:, cch * 512 + it * 128: cch * 512 + (it + 1) * 128]
                            tensor.matmul(pA[:, ybank:ybank + 512], lhs,
                                          w_qs[:, cch * 1024:cch * 1024 + 512],
                                          start=(cch == 0), stop=(cch == NCH - 1))
                            mm2 = tensor.matmul(pA[:, ybank + 512:ybank + 1024], lhs,
                                                w_qs[:, cch * 1024 + 512:(cch + 1) * 1024],
                                                start=(cch == 0), stop=(cch == NCH - 1))
                        ts.inc(mm2)
                        mmm.append(ts.n)
                        if it == 3:
                            smm.append(ts.n)
                    blk.tensor(_t)

                    def _vr(vector, t=t, tp=tp, buf=buf, xw=xw, ybank=ybank):
                        vector.wait_ge(ts.h, mmm[-1])
                        vector.wait_ge(dA[tp].h, dA[tp].n)
                        vector.wait_ge(dX[tp].h, dX[tp].n)
                        if ln_mode is not None:
                            lnsem = dL if ln_mode[1] is d_xln else dN
                            vector.wait_ge(lnsem[tp].h, lnsem[tp].n)
                        vs.inc(vector.scalar_tensor_tensor(
                            xw[:], pA[:, ybank:ybank + 1024], sqsy[:, 1:2], buf[:],
                            ALU.mult, ALU.add))
                        rm.append(vs.n)
                    blk.vector(_vr)

                    def _gp(gp, t=t, tp=tp, xw=xw):
                        gp.wait_ge(vs.h, rm[-1])
                        dX[tp].dma(gp.dma_start(xdst_dram[b, t * 128:(t + 1) * 128, :], xw[:]))
                    blk.gpsimd(_gp)

                    if ln_mode is not None:
                        mode, ln_dst = ln_mode
                        lnsem = dL if ln_dst is d_xln else dN
                        xnp = [xnp0, xnp1][tp]

                        def _vl1(vector, xw=xw, mode=mode):
                            ln_vec1(vector, xw[:], mode)
                        blk.vector(_vl1)

                        blk.scalar(ln_scalar)

                        def _vl2(vector, xw=xw, xnp=xnp):
                            ln_vec2(vector, xw[:], xnp[:])
                            rm[-1] = vs.n
                        blk.vector(_vl2)

                        def _gp2(gp, t=t, tp=tp, xnp=xnp, ln_dst=ln_dst, lnsem=lnsem):
                            gp.wait_ge(vs.h, rm[-1])
                            lnsem[tp].dma(gp.dma_start(
                                ln_dst[b, t * 128:(t + 1) * 128, :], xnp[:]))
                        blk.gpsimd(_gp2)

        def emit_attention(b):
            def _pre(sync):
                wait_stream_totals(sync, [dL])
                if mPV.n:
                    sync.wait_ge(mPV.h, mPV.n)
            blk.sync(_pre)

            if b > 0:
                def _gw(gp, tsn=ts.n):
                    gp.wait_ge(ts.h, tsn)
                    gld.dma(gp.dma_start(w_a[:].rearrange("p (k j) -> p k j", j=512),
                                         wq_d.rearrange("(k p) j -> p k j", p=128)))
                blk.gpsimd(_gw)

            def _vm(vector, tsn=ts.n):
                vector.wait_ge(ts.h, tsn)
                vs.inc(vector.memset(v_aug[:], 1.0))
            blk.vector(_vm)

            slot_cnt = [0]
            freed = []
            sq_marks = []
            tsq_mk = {}
            gld_gate = gld.n

            def emit_proj(I):
                p = I % 2
                sa = [aA0, aA1][p]
                pm = []
                for kind in range(12):
                    s = slot_cnt[0]
                    slot_cnt[0] += 1

                    def _t(tensor, I=I, p=p, sa=sa, kind=kind, s=s, gg=gld_gate,
                           wdv=wo_done_vs[0]):
                        if s == 0 and gg:
                            tensor.wait_ge(gld.h, gg)
                        if s == 0 and wdv:
                            tensor.wait_ge(vs.h, wdv)
                        tensor.wait_ge(dB[p].h, dB[p].n)
                        fm = mk(freed, s - 6)
                        if fm:
                            tensor.wait_ge(vs.h, fm)
                        if kind < 4:
                            wsb, pr = w_a, kind
                        elif kind < 8:
                            wsb, pr = wk_sb, kind - 4
                        else:
                            wsb, pr = wv_sb, kind - 8
                        for cch in range(NCH):
                            if kind < 8:
                                mm2 = tensor.matmul(
                                    pslot(s),
                                    wsb[:, cch * 512 + pr * 128: cch * 512 + (pr + 1) * 128],
                                    sa[:, cch * 512:(cch + 1) * 512],
                                    start=(cch == 0), stop=(cch == NCH - 1))
                            else:
                                mm2 = tensor.matmul(
                                    pslot(s),
                                    sa[:, cch * 512 + pr * 128: cch * 512 + (pr + 1) * 128],
                                    wv_sb[:, cch * 512:(cch + 1) * 512],
                                    start=(cch == 0), stop=(cch == NCH - 1))
                        ts.inc(mm2)
                        pm.append(ts.n)
                    blk.tensor(_t)

                    def _v(vector, I=I, kind=kind, s=s):
                        vector.wait_ge(ts.h, pm[-1])
                        if kind < 8:
                            dst_sb = qT_sb if kind < 4 else kT_sb
                            pr = kind % 4
                            dsl = dst_sb[:, pr * 4096 + I * 512: pr * 4096 + (I + 1) * 512]
                            vs.inc(vector.tensor_copy(dsl, pslot(s)))
                            freed.append(vs.n)
                            j = len(sq_marks)
                            tm = tsq_mk.get(j - 8, 0)
                            if tm:
                                vector.wait_ge(ts.h, tm)
                            vs.inc(vector.tensor_tensor(
                                ao_sb0[:, (j % 8) * 512: (j % 8 + 1) * 512],
                                dsl, dsl, ALU.mult))
                            sq_marks.append(vs.n)
                        else:
                            jcg = I * 4 + (kind - 8)
                            dst = v_aug[:, jcg * 520:(jcg + 1) * 520].rearrange(
                                "p (h x) -> p h x", x=65)[:, :, 0:64]
                            src = pslot(s).rearrange("p (h x) -> p h x", x=64)
                            vs.inc(vector.tensor_copy(dst, src))
                            freed.append(vs.n)
                    blk.vector(_v)
                return pm

            def emit_norm(I, sq_base):
                nm = []
                for half, col0 in ((0, 1024), (1, 1536)):
                    for pr in range(4):
                        j = sq_base + half * 4 + pr

                        def _t(tensor, half=half, col0=col0, pr=pr, j=j):
                            tensor.wait_ge(vs.h, sq_marks[j])
                            mm = tensor.matmul(pB[0:8, col0:col0 + 512],
                                               bones8[:, pr * 8:(pr + 1) * 8],
                                               ao_sb0[:, (j % 8) * 512: (j % 8 + 1) * 512],
                                               start=(pr == 0), stop=(pr == 3))
                            ts.inc(mm)
                            tsq_mk[j] = ts.n
                            nm.append(ts.n)
                        blk.tensor(_t)

                def _v(vector):
                    vector.wait_ge(ts.h, nm[-1])
                    vector.tensor_scalar(nrec[0:8, 0:512], pB[0:8, 1024:1536],
                                         1e-24, None, ALU.max)
                    vs.inc(vector.tensor_scalar(nrec[0:8, 512:1024], pB[0:8, 1536:2048],
                                                1e-24, None, ALU.max))
                    vector.wait_ge(vs.h, vs.n)
                    vector.reciprocal(nrec[0:8, 0:512], nrec[0:8, 0:512])
                    v2s.inc(vector.reciprocal(nrec[0:8, 512:1024], nrec[0:8, 512:1024]))
                blk.vector(_v)

                def _sc(scalar):
                    scalar.wait_ge(v2s.h, v2s.n)
                    s2v.inc(scalar.activation(rinv_n[0:8, :], nrec[0:8, :], AF.Sqrt))
                blk.scalar(_sc)

                bm = []
                for half in range(2):
                    for pr in range(4):
                        s = slot_cnt[0]
                        slot_cnt[0] += 1

                        def _t(tensor, half=half, pr=pr, s=s):
                            tensor.wait_ge(s2v.h, s2v.n)
                            fm = mk(freed, s - 6)
                            if fm:
                                tensor.wait_ge(vs.h, fm)
                            mm = tensor.matmul(pslot(s), sel8[0:8, pr * 128:(pr + 1) * 128],
                                               rinv_n[0:8, half * 512:(half + 1) * 512],
                                               start=True, stop=True)
                            ts.inc(mm)
                            bm.append(ts.n)
                        blk.tensor(_t)

                        def _v(vector, I=I, half=half, pr=pr, s=s):
                            vector.wait_ge(ts.h, bm[-1])
                            dst_sb = qT_sb if half == 0 else kT_sb
                            dsl = dst_sb[:, pr * 4096 + I * 512: pr * 4096 + (I + 1) * 512]
                            vs.inc(vector.tensor_tensor(dsl, dsl, pslot(s), ALU.mult))
                            freed.append(vs.n)
                        blk.vector(_v)

            pm_by_stripe = []
            for I in range(N_STRIPES):
                p = I % 2
                sa = [aA0, aA1][p]

                def _sy(sync, I=I, p=p, sa=sa):
                    if I >= 2:
                        sync.wait_ge(ts.h, pm_by_stripe[I - 2][-1])
                    for cch in range(NCH):
                        dB[p].dma(sync.dma_start(
                            sa[:, cch * 512:(cch + 1) * 512],
                            d_xln[b, I * 512:(I + 1) * 512, cch * 128:(cch + 1) * 128],
                            transpose=True))
                blk.sync(_sy)

                if I >= 1:
                    emit_norm(I - 1, (I - 1) * 8)
                pm_by_stripe.append(emit_proj(I))
            emit_norm(N_STRIPES - 1, (N_STRIPES - 1) * 8)

            def _gw2(gp, tsn=ts.n):
                gp.wait_ge(ts.h, tsn)
                gld.dma(gp.dma_start(w_a[:].rearrange("p (m u) -> p m u", u=1024),
                                     wo_d.rearrange("(m p) u -> p m u", p=128)))
            blk.gpsimd(_gw2)
            wo_gate = gld.n

            prep_vs = vs.n

            phx = [aA0[:, q * 1024:(q + 1) * 1024] for q in range(4)]
            nmm = []
            pend_t4 = [None]
            pend_pv = [None]

            def emit_pv_tail(tensor, args):
                m_p, eA_tgt = args
                tensor.wait_ge(eA.h, eA_tgt)
                pj = N_JC - 1
                phb = phx[pj % 4]
                va0 = pj * 520 + 130 * m_p
                tensor.matmul(pB[0:65, 0:512],
                              v_aug[:, va0:va0 + 65],
                              phb[:, 0:512],
                              start=False, stop=True)
                mm2 = tensor.matmul(pB[0:65, 512:1024],
                                    v_aug[:, va0 + 65:va0 + 130],
                                    phb[:, 512:1024],
                                    start=False, stop=True)
                mPV.inc(mm2)

            def emit_t4(tensor, args):
                rsc_mark = args[0]
                tensor.wait_ge(vs.h, rsc_mark)
                tensor.matmul(pB[0:64, 1024:1536], ones64[0:1, :],
                              rinv_ib[0:1, 0:512],
                              start=True, stop=True, tile_position=(0, 0))
                mm2 = tensor.matmul(pB[64:128, 1024:1536], ones64[0:1, :],
                                    rinv_ib[0:1, 512:1024],
                                    start=True, stop=True, tile_position=(0, 64))
                ts.inc(mm2)
                nmm.append(ts.n)

            def emit_v5(vector, paov, pi0, nm):
                vector.wait_ge(ts.h, nm)
                vs.inc(vector.reciprocal(pB[:, 1024:1536], pB[:, 1024:1536]))
                vector.wait_ge(vs.h, vs.n)
                cols = slice(pi0, pi0 + 512)
                evD.inc(vector.tensor_tensor(paov[:, cols], paov[:, cols],
                                             pB[:, 1024:1536], ALU.mult))

            for m in range(N_PAIRS):
                rsc_marks = []
                qv0 = m * 4096
                aov = ao_view(m)
                for ib8 in range(8):
                    i0 = ib8 * 512
                    eA_base = eA.n
                    mPV_base = mPV.n
                    mS_base = mS.n
                    evD_pre = evD.n

                    def _t(tensor, m=m, i0=i0, qv0=qv0, eA_base=eA_base, evD_pre=evD_pre,
                           first=(m == 0 and ib8 == 0), pt4=pend_t4[0], ppv=pend_pv[0]):
                        if first:
                            tensor.wait_ge(vs.h, prep_vs)
                        for jc in range(N_JC):
                            if jc == 1 and ppv is not None:
                                emit_pv_tail(tensor, ppv)
                            if jc == 3 and pt4 is not None:
                                emit_t4(tensor, pt4)
                            w = eA_base + jc - 1
                            if w > 0:
                                tensor.wait_ge(eA.h, w)
                            lg = pA[:, (jc % 2) * 1024:(jc % 2 + 1) * 1024]
                            tensor.matmul(lg[:, 0:512],
                                          kT_sb[0:64, qv0 + jc * 128: qv0 + (jc + 1) * 128],
                                          qT_sb[0:64, qv0 + i0: qv0 + i0 + 512],
                                          start=True, stop=True, tile_position=(0, 0))
                            mm2 = tensor.matmul(lg[:, 512:1024],
                                                kT_sb[64:128, qv0 + jc * 128: qv0 + (jc + 1) * 128],
                                                qT_sb[64:128, qv0 + i0: qv0 + i0 + 512],
                                                start=True, stop=True, tile_position=(64, 0))
                            mS.inc(mm2)
                            if jc > 0:
                                pj = jc - 1
                                if pj == 0:
                                    tensor.wait_ge(evD.h, evD_pre)
                                    if pt4 is not None:
                                        tensor.wait_ge(vs.h, pt4[0])
                                tensor.wait_ge(eA.h, eA_base + jc)
                                phb = phx[pj % 4]
                                va0 = pj * 520 + 130 * m
                                tensor.matmul(pB[0:65, 0:512],
                                              v_aug[:, va0:va0 + 65],
                                              phb[:, 0:512],
                                              start=(pj == 0), stop=False)
                                mm2 = tensor.matmul(pB[0:65, 512:1024],
                                                    v_aug[:, va0 + 65:va0 + 130],
                                                    phb[:, 512:1024],
                                                    start=(pj == 0), stop=False)
                                mPV.inc(mm2)
                    blk.tensor(_t)
                    pend_pv[0] = (m, eA_base + N_JC)

                    if pend_t4[0] is not None:
                        _, paov, pi0 = pend_t4[0]

                        def _v5p(vector, paov=paov, pi0=pi0, nm=nmm[-1]):
                            emit_v5(vector, paov, pi0, nm)
                        blk.vector(_v5p)

                    def _sc(scalar, eA_base=eA_base, mPV_base=mPV_base, mS_base=mS_base):
                        for jc in range(N_JC):
                            scalar.wait_ge(mS.h, mS_base + jc + 1)
                            w = mPV_base + jc - 2
                            if w > 0:
                                scalar.wait_ge(mPV.h, w)
                            lg = pA[:, (jc % 2) * 1024:(jc % 2 + 1) * 1024]
                            eA.inc(scalar.activation(phx[jc % 4], lg, AF.Exp,
                                                     scale=c16[:, 0:1], bias=cm16[:, 0:1]))
                    blk.scalar(_sc)

                    def _v(vector, m=m, i0=i0, aov=aov):
                        vector.wait_ge(mPV.h, mPV.n + (1 if pend_pv[0] is not None else 0))
                        cols = slice(i0, i0 + 512)
                        vector.tensor_copy(aov[0:64, cols], pB[0:64, 0:512])
                        vector.tensor_copy(aov[64:128, cols], pB[0:64, 512:1024])
                        vector.tensor_scalar(rinv_ib[0:1, 0:512], pB[64:65, 0:512],
                                             1e-6, None, ALU.max)
                        vs.inc(vector.tensor_scalar(rinv_ib[0:1, 512:1024],
                                                    pB[64:65, 512:1024],
                                                    1e-6, None, ALU.max))
                        rsc_marks.append(vs.n)
                    blk.vector(_v)

                    pend_t4[0] = (rsc_marks[-1], aov, i0)

            lrsc, laov, li0 = pend_t4[0]
            pend_t4[0] = None
            lppv = pend_pv[0]
            pend_pv[0] = None

            def _t4f(tensor):
                emit_pv_tail(tensor, lppv)
                emit_t4(tensor, (lrsc,))
            blk.tensor(_t4f)

            def _v5f(vector):
                emit_v5(vector, laov, li0, nmm[-1])
            blk.vector(_v5f)

            mmm = []
            rm = []

            def _pre2(sync):
                wait_stream_totals(sync, [dX])
            blk.sync(_pre2)

            for t in range(N_TILES):
                tp = t % 2
                buf = [xsrc0, xsrc1][tp]
                xw = [xw0, xw1][tp]
                ybank = tp * 1024

                def _sy(sync, t=t, tp=tp, buf=buf):
                    if t >= 2:
                        sync.wait_ge(vs.h, mk(rm, t - 2))
                    dA[tp].dma(sync.dma_start(buf[:], d_x1[b, t * 128:(t + 1) * 128, :]))
                blk.sync(_sy)

                def _t(tensor, t=t, ybank=ybank):
                    tensor.wait_ge(evD.h, evD.n)
                    if t == 0:
                        tensor.wait_ge(gld.h, wo_gate)
                    if t >= 2:
                        tensor.wait_ge(vs.h, mk(rm, t - 2))
                    for mmi in range(4):
                        lhs = ao_view(mmi)[:, t * 128:(t + 1) * 128]
                        tensor.matmul(pA[:, ybank:ybank + 512], lhs,
                                      w_a[:, mmi * 1024:mmi * 1024 + 512],
                                      start=(mmi == 0), stop=(mmi == 3))
                        mm2 = tensor.matmul(pA[:, ybank + 512:ybank + 1024], lhs,
                                            w_a[:, mmi * 1024 + 512:(mmi + 1) * 1024],
                                            start=(mmi == 0), stop=(mmi == 3))
                    ts.inc(mm2)
                    mmm.append(ts.n)
                blk.tensor(_t)

                def _v(vector, t=t, tp=tp, buf=buf, xw=xw, ybank=ybank):
                    vector.wait_ge(ts.h, mmm[-1])
                    vector.wait_ge(dA[tp].h, dA[tp].n)
                    vector.wait_ge(dX[tp].h, dX[tp].n)
                    if STAGE >= 3:
                        vector.wait_ge(dN[tp].h, dN[tp].n)
                    vs.inc(vector.tensor_tensor(xw[:], pA[:, ybank:ybank + 1024], buf[:], ALU.add))
                    rm.append(vs.n)
                blk.vector(_v)

                def _gp(gp, t=t, tp=tp, xw=xw):
                    gp.wait_ge(vs.h, rm[-1])
                    dX[tp].dma(gp.dma_start(
                        (d_x2 if STAGE >= 3 else out_d)[b, t * 128:(t + 1) * 128, :], xw[:]))
                blk.gpsimd(_gp)

                if STAGE >= 3:
                    xnp = [xnp0, xnp1][tp]

                    def _vl1(vector, xw=xw):
                        ln_vec1(vector, xw[:], "ffn")
                    blk.vector(_vl1)

                    blk.scalar(ln_scalar)

                    def _vl2(vector, xw=xw, xnp=xnp):
                        ln_vec2(vector, xw[:], xnp[:])
                        rm[-1] = vs.n
                    blk.vector(_vl2)

                    def _gp2(gp, t=t, tp=tp, xnp=xnp):
                        gp.wait_ge(vs.h, rm[-1])
                        dN[tp].dma(gp.dma_start(d_xnt[b, t * 128:(t + 1) * 128, :], xnp[:]))
                    blk.gpsimd(_gp2)
            wo_done_vs[0] = vs.n

        emit_init()
        init_gate()
        fence()

        for b in range(B_PER_CORE):
            emit_ffn_A(b, x_in, "ffn")
            fence()
        for b in range(B_PER_CORE):
            emit_ffn_B1(b, cmcol0=b * 8)
            fence()
        emit_gamma(cc_in1, cc_out1)
        for b in range(B_PER_CORE):
            emit_ffn_B2(b, x_in, out_d if STAGE == 1 else d_x1,
                        None if STAGE == 1 else ("plain", d_xln), res_is_x2=False)
            fence()

        def emit_ffn_B1_weave(b, cmcol0):
            b1m = []
            stq = {}

            def _gate(sync, dNn=dN[0].n + dN[1].n):
                for s in dN + dX:
                    if s.n:
                        sync.wait_ge(s.h, s.n)
                sync.wait_ge(vs.h, vs.n)
            blk.sync(_gate)

            for i in range(16):
                cch, hf = i // 2, i % 2
                p = i % 2
                scr = [xw0, xw1][p][:].bitcast(BF16)

                def _sy(sync, b=b, i=i, cch=cch, hf=hf, p=p, scr=scr):
                    if i >= 2:
                        sync.wait_ge(vs.h, b1m[i - 2])
                        w = stq.get(i - 2)
                        if w:
                            sync.wait_ge(dT[p].h, w)
                    dB[p].dma(sync.dma_start(
                        scr, d_xnt[b, hf * 2048:(hf + 1) * 2048,
                                   cch * 128:(cch + 1) * 128],
                        transpose=True))
                blk.sync(_sy)

                def _v(vector, i=i, cch=cch, hf=hf, p=p, scr=scr, cmcol0=cmcol0):
                    vector.wait_ge(dB[p].h, dB[p].n)
                    vs.inc(vector.tensor_scalar(scr, scr, g_sb[:, cch:cch + 1],
                                                b_sb[:, cch:cch + 1], ALU.mult, ALU.add))
                    vector.wait_ge(vs.h, vs.n)
                    dst = chmax[:, cmcol0 + cch:cmcol0 + cch + 1] if hf == 0 \
                        else sabs[:, 0:1]
                    vs.inc(vector.tensor_reduce(dst, scr, axis=AX, op=ALU.max,
                                                apply_absolute_value=True))
                    if hf == 1:
                        vector.wait_ge(vs.h, vs.n)
                        vs.inc(vector.tensor_tensor(
                            chmax[:, cmcol0 + cch:cmcol0 + cch + 1],
                            chmax[:, cmcol0 + cch:cmcol0 + cch + 1],
                            sabs[:, 0:1], ALU.max))
                    b1m.append(vs.n)
                blk.vector(_v)

                def _gp(gp, b=b, i=i, cch=cch, hf=hf, p=p, scr=scr):
                    gp.wait_ge(vs.h, b1m[-1])
                    dT[p].dma(gp.dma_start(
                        d_xnT[b, cch * 128:(cch + 1) * 128,
                              hf * 2048:(hf + 1) * 2048], scr))
                    stq[i] = dT[p].n
                blk.gpsimd(_gp)

        if STAGE >= 2:
            emit_attention(0)
            if STAGE >= 3:
                emit_ffn_B1_weave(0, 0)
            emit_attention(1)
            fence()

        if STAGE >= 3:
            emit_ffn_B1_weave(1, 8)
            fence()
            emit_gamma(cc_in3, cc_out3)
            for b in range(B_PER_CORE):
                emit_ffn_B2(b, d_x2, out_d, None, res_is_x2=True)
                fence()

        def _fin_g(gp):
            for s2 in (dN, dL, dX, dT):
                for s in s2:
                    if s.n:
                        gp.wait_ge(s.h, s.n)
        blk.gpsimd(_fin_g)

        def _fin_s(sync):
            for s in dX:
                if s.n:
                    sync.wait_ge(s.h, s.n)
        blk.sync(_fin_s)

    return nc


def _consts():
    b8 = np.zeros((128, 32), dtype=bf16)
    for m in range(4):
        b8[0:64, m * 8 + 2 * m] = 1
        b8[64:128, m * 8 + 2 * m + 1] = 1
    s8 = np.zeros((8, 512), dtype=bf16)
    for m in range(4):
        for c in range(128):
            s8[2 * m + c // 64, m * 128 + c] = 1
    return dict(c_bones8=b8, c_sel8=s8,
                c_ones64=np.ones((1, 64), dtype=bf16),
                c_ones1f=np.ones((1, 128), dtype=np.float32),
                c_ones128f=np.ones((128, 1), dtype=np.float32))


def kernel(L0_attn_x, L0_attn_Wq, L0_attn_Wkv, L0_attn_Wo,
           L0_ffn_weight, L0_ffn_ln_gamma, L0_ffn_ln_beta):
    x = np.ascontiguousarray(np.asarray(L0_attn_x, dtype=np.float32))
    wq = np.ascontiguousarray(np.asarray(L0_attn_Wq, dtype=np.float32))
    wkv = np.ascontiguousarray(np.asarray(L0_attn_Wkv, dtype=np.float32))
    wo = np.ascontiguousarray(np.asarray(L0_attn_Wo, dtype=np.float32))
    w = np.ascontiguousarray(np.asarray(L0_ffn_weight, dtype=np.float32))
    lg = np.ascontiguousarray(np.asarray(L0_ffn_ln_gamma, dtype=np.float32))
    lb = np.ascontiguousarray(np.asarray(L0_ffn_ln_beta, dtype=np.float32))

    consts = _consts()
    nc = build_program()
    in_maps = []
    for c in range(N_CORES):
        m = dict(x_in=x[c * B_PER_CORE:(c + 1) * B_PER_CORE],
                 wq_d=wq, wkv_d=wkv, wo_d=wo, w_d=w, lg_d=lg, lb_d=lb)
        m.update(consts)
        in_maps.append(m)
    res = run_bass_kernel_spmd(nc, in_maps, core_ids=list(range(N_CORES)))
    return np.concatenate([r["out"] for r in res.results], axis=0)


# version 25
# speedup vs baseline: 1.0217x; 1.0217x over previous
import os
import numpy as np
import ml_dtypes
import concourse.bass as bass
import concourse.mybir as mybir
from concourse.bass_utils import run_bass_kernel_spmd

F32 = mybir.dt.float32
BF16 = mybir.dt.bfloat16
AF = mybir.ActivationFunctionType
ALU = mybir.AluOpType
AX = mybir.AxisListType.X

N_CORES = 8
B_PER_CORE = 2
N_TOK = 4096
D = 1024
NCH = 8
N_TILES = 32
N_STRIPES = 8
N_PAIRS = 4
IB = 1024
N_IB = 4
N_JC = 32
QB = 128.0
CLIP_HI = float(np.float32(128.0) - np.float32(1e-5))
CLIP_LO = float(np.float32(-128.0) + np.float32(1e-5))
STAGE = int(os.environ.get("KSTAGE", "3"))

bf16 = ml_dtypes.bfloat16


class Sem:
    def __init__(self, nc, es, name):
        self.h = es.enter_context(nc.semaphore(name))
        self.n = 0

    def inc(self, instr, k=1):
        instr.then_inc(self.h, k)
        self.n += k
        return instr

    def dma(self, instr):
        instr.then_inc(self.h, 16)
        self.n += 16
        return instr


def mk(lst, j):
    return lst[j] if 0 <= j < len(lst) else 0


def build_program():
    nc = bass.Bass("TRN2", target_bir_lowering=False)

    x_in = nc.dram_tensor("x_in", [B_PER_CORE, N_TOK, D], F32, kind="ExternalInput").ap()
    wq_d = nc.dram_tensor("wq_d", [D, 512], F32, kind="ExternalInput").ap()
    wkv_d = nc.dram_tensor("wkv_d", [D, 1024], F32, kind="ExternalInput").ap()
    wo_d = nc.dram_tensor("wo_d", [512, D], F32, kind="ExternalInput").ap()
    w_d = nc.dram_tensor("w_d", [D, D], F32, kind="ExternalInput").ap()
    lg_d = nc.dram_tensor("lg_d", [D], F32, kind="ExternalInput").ap()
    lb_d = nc.dram_tensor("lb_d", [D], F32, kind="ExternalInput").ap()
    c_bones8 = nc.dram_tensor("c_bones8", [128, 32], BF16, kind="ExternalInput").ap()
    c_sel8 = nc.dram_tensor("c_sel8", [8, 512], BF16, kind="ExternalInput").ap()
    c_ones64 = nc.dram_tensor("c_ones64", [1, 64], BF16, kind="ExternalInput").ap()
    c_ones1f = nc.dram_tensor("c_ones1f", [1, 128], F32, kind="ExternalInput").ap()
    c_ones128f = nc.dram_tensor("c_ones128f", [128, 1], F32, kind="ExternalInput").ap()

    out_d = nc.dram_tensor("out", [B_PER_CORE, N_TOK, D], F32, kind="ExternalOutput").ap()

    d_xnt = nc.dram_tensor("d_xnt", [B_PER_CORE, N_TOK, D], BF16).ap()
    d_xnT = nc.dram_tensor("d_xnT", [B_PER_CORE, D, N_TOK], BF16).ap()
    d_x1 = nc.dram_tensor("d_x1", [B_PER_CORE, N_TOK, D], F32).ap()
    d_x2 = nc.dram_tensor("d_x2", [B_PER_CORE, N_TOK, D], F32).ap()
    d_xln = nc.dram_tensor("d_xln", [B_PER_CORE, N_TOK, D], BF16).ap()
    d_gred = nc.dram_tensor("d_gred", [128, 1], F32).ap()
    cc_in1 = nc.dram_tensor("cc_in1", [1, 1], F32).ap()
    cc_out1 = nc.dram_tensor("cc_out1", [1, 1], F32).ap()
    cc_in3 = nc.dram_tensor("cc_in3", [1, 1], F32).ap()
    cc_out3 = nc.dram_tensor("cc_out3", [1, 1], F32).ap()

    from contextlib import ExitStack
    es = ExitStack()
    S = lambda name, shape, dt: es.enter_context(nc.sbuf_tensor(name, shape, dt))
    with es:
        w_qs = S("w_qs", [128, 8 * 1024], BF16)
        w_a = S("w_a", [128, 4 * 1024], BF16)
        wk_sb = S("wk_sb", [128, 8 * 512], BF16)
        wv_sb = S("wv_sb", [128, 8 * 512], BF16)
        g_sb = S("g_sb", [128, 8], F32)
        b_sb = S("b_sb", [128, 8], F32)
        bones8 = S("bones8", [128, 32], BF16)
        sel8 = S("sel8", [8, 512], BF16)
        ones64 = S("ones64", [1, 64], BF16)
        ones1f = S("ones1f", [1, 128], F32)
        ones128f = S("ones128f", [128, 1], F32)
        c16 = S("c16", [128, 1], F32)
        cm16 = S("cm16", [128, 1], F32)
        sabs = S("sabs", [128, 8], F32)
        tny = S("tny", [1, 8], F32)
        t128 = S("t128", [128, 2], F32)
        qT_sb = S("qT_sb", [128, 4 * 4096], BF16)
        kT_sb = S("kT_sb", [128, 4 * 4096], BF16)
        v_aug = S("v_aug", [128, N_JC * 520], BF16)
        ao_sb0 = S("ao_sb0", [128, 4096], BF16)
        aA0 = S("aA0", [128, 4096], BF16)
        aA1 = S("aA1", [128, 4096], BF16)
        xsrc0 = S("xsrc0", [128, 1024], F32)
        xsrc1 = S("xsrc1", [128, 1024], F32)
        xw0 = S("xw0", [128, 1024], F32)
        xw1 = S("xw1", [128, 1024], F32)
        xnp0 = S("xnp0", [128, 1024], BF16)
        xnp1 = S("xnp1", [128, 1024], BF16)
        bnt = S("bnt", [128, 12], F32)
        mv = S("mv", [128, 2], F32)
        stt = S("sttt", [128, 8], F32)
        chmax = S("chmax", [128, 16], F32)
        lmax = S("lmax", [128, 1], F32)
        grow = S("grow", [1, 128], F32)
        g11 = S("g11", [1, 4], F32)
        scpack = S("scpack", [1, 2], F32)
        sqsy = S("sqsy", [128, 2], F32)
        nrec = S("nrec", [8, 1024], F32)
        rinv_n = S("rinv_n", [8, 1024], BF16)
        rs_ib = S("rs_ib", [1, 2048], F32)
        rinv_ib = S("rinv_ib", [1, 2048], BF16)

        ph0 = aA0[:, 0:2048]
        ph1 = aA0[:, 2048:4096]
        sq0 = kT_sb[:, 0:4096]
        sq1 = kT_sb[:, 4096:8192]
        tmpg = ao_sb0[:, 0:1024]
        tmpl = ao_sb0[:, 1024:2048]
        wst = xw0

        def ao_view(m):
            if m == 0:
                return ao_sb0
            return qT_sb[:, (m - 1) * 4096: m * 4096]

        pA = es.enter_context(nc.psum_tensor("pA", [128, 2048], F32))
        pB = es.enter_context(nc.psum_tensor("pB", [128, 2048], F32))

        def pslot(s):
            s = s % 6
            if s < 4:
                return pA[:, s * 512:(s + 1) * 512]
            return pB[:, (s - 4) * 512:(s - 4 + 1) * 512]

        sm = {}
        for name in ["ld", "gld", "vs", "ss", "ts", "ph",
                     "dA0", "dA1", "dB0", "dB1", "dN0", "dN1",
                     "dL0", "dL1", "dX0", "dX1", "dT0", "dT1",
                     "v2s", "s2v", "eA", "mS", "mPV", "evD",
                     "cg1", "cg2", "cg3", "cg4", "ccc"]:
            sm[name] = Sem(nc, es, name)
        ld, gld, vs, ss, ts, ph = (sm[k] for k in ["ld", "gld", "vs", "ss", "ts", "ph"])
        dA = (sm["dA0"], sm["dA1"])
        dB = (sm["dB0"], sm["dB1"])
        dN = (sm["dN0"], sm["dN1"])
        dL = (sm["dL0"], sm["dL1"])
        dX = (sm["dX0"], sm["dX1"])
        dT = (sm["dT0"], sm["dT1"])
        v2s, s2v, eA, mS, mPV, evD = (sm[k] for k in
                                      ["v2s", "s2v", "eA", "mS", "mPV", "evD"])
        cg1, cg2, cg3, cg4, ccc = (sm[k] for k in ["cg1", "cg2", "cg3", "cg4", "ccc"])

        blk = es.enter_context(nc.Block())
        ENGS = ["sync", "vector", "scalar", "tensor", "gpsimd"]
        wo_done_vs = [0]

        def fence():
            for e in ENGS:
                def _f(eng):
                    eng.nop().then_inc(ph.h, 1)
                getattr(blk, e)(_f)
            ph.n += len(ENGS)
            tgt = ph.n
            for e in ENGS:
                def _g(eng, tgt=tgt):
                    eng.wait_ge(ph.h, tgt)
                getattr(blk, e)(_g)

        def init_gate():
            vals = [(ld, ld.n), (gld, gld.n), (vs, vs.n), (ts, ts.n)]
            for e in ENGS:
                def _g(eng, vals=vals):
                    for s, v in vals:
                        if v:
                            eng.wait_ge(s.h, v)
                getattr(blk, e)(_g)

        def wait_stream_totals(eng, streams):
            for s2 in streams:
                for s in s2:
                    if s.n:
                        eng.wait_ge(s.h, s.n)

        def emit_init():
            def _sync(sync):
                sync.dma_start(bones8[:], c_bones8).then_inc(ld.h, 16)
                sync.dma_start(sel8[:], c_sel8).then_inc(ld.h, 16)
                sync.dma_start(ones64[:], c_ones64).then_inc(ld.h, 16)
                sync.dma_start(ones1f[:], c_ones1f).then_inc(ld.h, 16)
                sync.dma_start(ones128f[:], c_ones128f).then_inc(ld.h, 16)
                sync.dma_start(g_sb[:], lg_d.rearrange("(k p) -> p k", p=128),
                               allow_slow_non_contiguous=True).then_inc(ld.h, 16)
                sync.dma_start(b_sb[:], lb_d.rearrange("(k p) -> p k", p=128),
                               allow_slow_non_contiguous=True).then_inc(ld.h, 16)
                ld.n += 7 * 16
                for cch in range(8):
                    if cch >= 1:
                        sync.wait_ge(vs.h, cch)
                    sync.dma_start(wst[:], w_d[cch * 128:(cch + 1) * 128, :]).then_inc(ld.h, 16)
                    ld.n += 16
            blk.sync(_sync)

            def _gp(gp):
                gp.dma_start(w_a[:].rearrange("p (k j) -> p k j", j=512),
                             wq_d.rearrange("(k p) j -> p k j", p=128)).then_inc(gld.h, 16)
                gp.dma_start(wk_sb[:].rearrange("p (k j) -> p k j", j=512),
                             wkv_d[:, 0:512].rearrange("(k p) j -> p k j", p=128)).then_inc(gld.h, 16)
                gp.dma_start(wv_sb[:].rearrange("p (k j) -> p k j", j=512),
                             wkv_d[:, 512:1024].rearrange("(k p) j -> p k j", p=128)).then_inc(gld.h, 16)
                gld.n += 3 * 16
                gp.wait_ge(gld.h, gld.n)
            blk.gpsimd(_gp)

            def _v1(vector):
                vector.memset(c16[:], 16.0)
                vector.memset(cm16[:], -16.0)
                for cch in range(8):
                    vector.wait_ge(ld.h, (7 + cch + 1) * 16)
                    vs.inc(vector.tensor_reduce(sabs[:, cch:cch + 1], wst[:],
                                                axis=AX, op=ALU.add, apply_absolute_value=True))
            blk.vector(_v1)

            def _t1(tensor):
                tensor.wait_ge(ld.h, ld.n)
                tensor.wait_ge(vs.h, vs.n)
                ts.inc(tensor.matmul(pB[0:1, 0:8], ones128f[:, 0:1], sabs[:],
                                     start=True, stop=True))
            blk.tensor(_t1)

            def _v2(vector):
                vector.wait_ge(ts.h, ts.n)
                vs.inc(vector.tensor_reduce(tny[0:1, 0:1], pB[0:1, 0:8], axis=AX, op=ALU.add))
                vector.wait_ge(vs.h, vs.n)
                vs.inc(vector.tensor_scalar(tny[0:1, 1:2], tny[0:1, 0:1],
                                            1.0 / float(D * D), None, ALU.mult))
                vector.wait_ge(vs.h, vs.n)
                vector.tensor_scalar(tny[0:1, 2:3], tny[0:1, 1:2], 1e-5, 0.5, ALU.add, ALU.mult)
                vs.inc(vector.tensor_scalar(tny[0:1, 4:5], tny[0:1, 1:2], 0.5 / QB, None, ALU.mult))
                vector.wait_ge(vs.h, vs.n)
                vs.inc(vector.tensor_scalar(tny[0:1, 3:4], tny[0:1, 2:3], -1.0, None, ALU.mult))
            blk.vector(_v2)

            def _t2(tensor):
                tensor.wait_ge(vs.h, vs.n)
                ts.inc(tensor.matmul(pB[:, 512:514], ones1f[0:1, :], tny[0:1, 2:4],
                                     start=True, stop=True))
            blk.tensor(_t2)

            def _v3(vector):
                vector.wait_ge(ts.h, ts.n)
                vs.inc(vector.tensor_copy(t128[:], pB[:, 512:514]))
            blk.vector(_v3)

            vbase = [0]

            def _sync2(sync):
                sync.wait_ge(vs.h, vs.n)
                vbase[0] = vs.n
                for cch in range(8):
                    if cch >= 1:
                        sync.wait_ge(vs.h, vbase[0] + 2 * cch)
                    sync.dma_start(wst[:], w_d[cch * 128:(cch + 1) * 128, :]).then_inc(ld.h, 16)
                    ld.n += 16
            blk.sync(_sync2)

            def _v4(vector):
                base = ld.n - 8 * 16
                vector.wait_ge(vs.h, vs.n)
                for cch in range(8):
                    vector.wait_ge(ld.h, base + (cch + 1) * 16)
                    vector.tensor_scalar(tmpg, wst[:], t128[:, 0:1], None, ALU.is_gt)
                    vs.inc(vector.tensor_scalar(tmpl, wst[:], t128[:, 1:2], None, ALU.is_lt))
                    vector.wait_ge(vs.h, vs.n)
                    vs.inc(vector.tensor_tensor(w_qs[:, cch * 1024:(cch + 1) * 1024],
                                                tmpg, tmpl, ALU.subtract))
            blk.vector(_v4)

        def ln_vec1(vector, src_ap, mode):
            vector.wait_ge(vs.h, vs.n)
            vector.bn_stats(bnt[:, 0:6], src_ap[:, 0:512])
            vs.inc(vector.bn_stats(bnt[:, 6:12], src_ap[:, 512:1024]))
            vector.wait_ge(vs.h, vs.n)
            vs.inc(vector.bn_aggr(mv[:], bnt[:]))
            vector.wait_ge(vs.h, vs.n)
            vector.tensor_scalar(stt[:, 0:1], mv[:, 0:1], -1.0, None, ALU.mult)
            if mode == "ffn":
                vs.inc(vector.tensor_scalar(stt[:, 1:2], mv[:, 1:2], 1.001, 1e-8, ALU.mult, ALU.add))
            else:
                vs.inc(vector.tensor_scalar(stt[:, 1:2], mv[:, 1:2], 1e-5, None, ALU.add))
            vector.wait_ge(vs.h, vs.n)
            v2s.inc(vector.reciprocal(stt[:, 2:3], stt[:, 1:2]))

        def ln_scalar(scalar):
            scalar.wait_ge(v2s.h, v2s.n)
            s2v.inc(scalar.activation(stt[:, 3:4], stt[:, 2:3], AF.Sqrt))

        def ln_vec2(vector, src_ap, xnp_ap):
            vector.wait_ge(s2v.h, s2v.n)
            vs.inc(vector.tensor_scalar(xnp_ap, src_ap, stt[:, 0:1], stt[:, 3:4],
                                        ALU.add, ALU.mult))

        def emit_ffn_A(b, src_dram, mode):
            aff = []
            for t in range(N_TILES):
                p = t % 2
                buf = [xsrc0, xsrc1][p]
                xnp = [xnp0, xnp1][p]

                def _sy(sync, t=t, p=p, buf=buf):
                    if t >= 2:
                        sync.wait_ge(vs.h, mk(aff, t - 2))
                    dA[p].dma(sync.dma_start(buf[:], src_dram[b, t * 128:(t + 1) * 128, :]))
                blk.sync(_sy)

                def _v1(vector, p=p, buf=buf):
                    vector.wait_ge(dA[p].h, dA[p].n)
                    ln_vec1(vector, buf[:], mode)
                blk.vector(_v1)

                blk.scalar(ln_scalar)

                def _v2(vector, t=t, p=p, buf=buf, xnp=xnp):
                    vector.wait_ge(dN[p].h, dN[p].n)
                    ln_vec2(vector, buf[:], xnp[:])
                    aff.append(vs.n)
                blk.vector(_v2)

                def _gp(gp, t=t, p=p, xnp=xnp):
                    gp.wait_ge(vs.h, aff[-1])
                    dN[p].dma(gp.dma_start(d_xnt[b, t * 128:(t + 1) * 128, :], xnp[:]))
                blk.gpsimd(_gp)

        def emit_ffn_B1(b, cmcol0):
            done = []

            def _pre(sync):
                wait_stream_totals(sync, [dN])
                if sm["mPV"].n:
                    sync.wait_ge(sm["mPV"].h, sm["mPV"].n)
            blk.sync(_pre)

            for k in range(NCH):
                p = k % 2
                buf = [aA0, aA1][p]

                def _sy(sync, k=k, p=p, buf=buf):
                    sync.wait_ge(dT[p].h, dT[p].n)
                    dB[p].dma(sync.dma_start(buf[:], d_xnt[b, :, k * 128:(k + 1) * 128],
                                             transpose=True))
                blk.sync(_sy)

                def _v(vector, k=k, p=p, buf=buf):
                    vector.wait_ge(dB[p].h, dB[p].n)
                    vs.inc(vector.tensor_scalar(buf[:], buf[:], g_sb[:, k:k + 1],
                                                b_sb[:, k:k + 1], ALU.mult, ALU.add))
                    vector.wait_ge(vs.h, vs.n)
                    vs.inc(vector.tensor_reduce(chmax[:, cmcol0 + k:cmcol0 + k + 1], buf[:],
                                                axis=AX, op=ALU.max, apply_absolute_value=True))
                    done.append(vs.n)
                blk.vector(_v)

                def _gp(gp, k=k, p=p, buf=buf):
                    gp.wait_ge(vs.h, done[-1])
                    dT[p].dma(gp.dma_start(d_xnT[b, k * 128:(k + 1) * 128, :], buf[:]))
                blk.gpsimd(_gp)

        def emit_gamma(ccin, ccout):
            def _v1(vector):
                vector.wait_ge(vs.h, vs.n)
                vs.inc(vector.tensor_reduce(lmax[:], chmax[:], axis=AX, op=ALU.max))
            blk.vector(_v1)

            def _gp1(gp):
                gp.wait_ge(vs.h, vs.n)
                cg1.dma(gp.dma_start(d_gred, lmax[:]))
            blk.gpsimd(_gp1)

            def _sy(sync):
                sync.wait_ge(cg1.h, cg1.n)
                cg2.dma(sync.dma_start(grow[:], d_gred.rearrange("p one -> one p")))
            blk.sync(_sy)

            def _v2(vector):
                vector.wait_ge(cg2.h, cg2.n)
                vs.inc(vector.tensor_reduce(g11[0:1, 0:1], grow[:], axis=AX, op=ALU.max))
            blk.vector(_v2)

            def _sc1(scalar):
                scalar.wait_ge(vs.h, vs.n)
                cg3.dma(scalar.dma_start(ccin, g11[0:1, 0:1]))
            blk.scalar(_sc1)

            def _gp2(gp):
                gp.wait_ge(cg3.h, cg3.n)
                ccc.inc(gp.collective_compute(
                    "AllReduce", ALU.max, replica_groups=[list(range(N_CORES))],
                    ins=[ccin], outs=[ccout]), 1)
            blk.gpsimd(_gp2)

            def _sc2(scalar):
                scalar.wait_ge(ccc.h, ccc.n)
                cg4.dma(scalar.dma_start(g11[0:1, 1:2], ccout))
            blk.scalar(_sc2)

            def _v3(vector):
                vector.wait_ge(cg4.h, cg4.n)
                vs.inc(vector.reciprocal(g11[0:1, 2:3], g11[0:1, 1:2]))
                vector.wait_ge(vs.h, vs.n)
                vector.tensor_scalar(scpack[0:1, 0:1], g11[0:1, 2:3], QB, None, ALU.mult)
                vs.inc(vector.tensor_scalar(scpack[0:1, 1:2], g11[0:1, 1:2], tny[0:1, 4:5],
                                            None, ALU.mult))
            blk.vector(_v3)

            def _t(tensor):
                tensor.wait_ge(vs.h, vs.n)
                ts.inc(tensor.matmul(pB[:, 1024:1026], ones1f[0:1, :], scpack[0:1, :],
                                     start=True, stop=True))
            blk.tensor(_t)

            def _v4(vector):
                vector.wait_ge(ts.h, ts.n)
                vs.inc(vector.tensor_copy(sqsy[:], pB[:, 1024:1026]))
            blk.vector(_v4)

        def emit_ffn_B2(b, xres_dram, xdst_dram, ln_mode, res_is_x2):
            qm = []
            mmm = []
            smm = []
            rm = []

            def _pre(sync):
                wait_stream_totals(sync, [dT] + ([dX] if res_is_x2 else []))
                if sm["mPV"].n:
                    sync.wait_ge(sm["mPV"].h, sm["mPV"].n)
            blk.sync(_pre)

            for I in range(N_STRIPES):
                p = I % 2
                sa = [aA0, aA1][p]
                sq = [sq0, sq1][p]

                def _sy1(sync, I=I, p=p, sa=sa):
                    if I >= 2:
                        sync.wait_ge(vs.h, mk(qm, I - 2))
                    for cch in range(NCH):
                        dB[p].dma(sync.dma_start(
                            sa[:, cch * 512:(cch + 1) * 512],
                            d_xnT[b, cch * 128:(cch + 1) * 128, I * 512:(I + 1) * 512]))
                blk.sync(_sy1)

                def _vq(vector, I=I, p=p, sa=sa, sq=sq):
                    vector.wait_ge(vs.h, vs.n)
                    vector.wait_ge(dB[p].h, dB[p].n)
                    if I >= 2:
                        vector.wait_ge(ts.h, mk(smm, I - 2))
                    vs.inc(vector.tensor_scalar(sq, sa[:], sqsy[:, 0:1], CLIP_HI,
                                                ALU.mult, ALU.min))
                    vector.wait_ge(vs.h, vs.n)
                    vs.inc(vector.tensor_scalar(sq, sq, CLIP_LO, None, ALU.max))
                    qm.append(vs.n)
                blk.vector(_vq)

                for it in range(4):
                    t = I * 4 + it
                    tp = t % 2
                    buf = [xsrc0, xsrc1][tp]
                    xw = [xw0, xw1][tp]
                    ybank = tp * 1024

                    def _sy2(sync, t=t, tp=tp, buf=buf):
                        if t >= 2:
                            sync.wait_ge(vs.h, mk(rm, t - 2))
                        dA[tp].dma(sync.dma_start(buf[:],
                                                  xres_dram[b, t * 128:(t + 1) * 128, :]))
                    blk.sync(_sy2)

                    def _t(tensor, t=t, sq=sq, it=it, ybank=ybank, ean=eA.n):
                        if t == 0 and ean:
                            tensor.wait_ge(eA.h, ean)
                        tensor.wait_ge(vs.h, qm[-1])
                        if t >= 2:
                            tensor.wait_ge(vs.h, mk(rm, t - 2))
                        for cch in range(NCH):
                            lhs = sq[:, cch * 512 + it * 128: cch * 512 + (it + 1) * 128]
                            tensor.matmul(pA[:, ybank:ybank + 512], lhs,
                                          w_qs[:, cch * 1024:cch * 1024 + 512],
                                          start=(cch == 0), stop=(cch == NCH - 1))
                            mm2 = tensor.matmul(pA[:, ybank + 512:ybank + 1024], lhs,
                                                w_qs[:, cch * 1024 + 512:(cch + 1) * 1024],
                                                start=(cch == 0), stop=(cch == NCH - 1))
                        ts.inc(mm2)
                        mmm.append(ts.n)
                        if it == 3:
                            smm.append(ts.n)
                    blk.tensor(_t)

                    def _vr(vector, t=t, tp=tp, buf=buf, xw=xw, ybank=ybank):
                        vector.wait_ge(ts.h, mmm[-1])
                        vector.wait_ge(dA[tp].h, dA[tp].n)
                        vector.wait_ge(dX[tp].h, dX[tp].n)
                        if ln_mode is not None:
                            lnsem = dL if ln_mode[1] is d_xln else dN
                            vector.wait_ge(lnsem[tp].h, lnsem[tp].n)
                        vs.inc(vector.scalar_tensor_tensor(
                            xw[:], pA[:, ybank:ybank + 1024], sqsy[:, 1:2], buf[:],
                            ALU.mult, ALU.add))
                        rm.append(vs.n)
                    blk.vector(_vr)

                    def _gp(gp, t=t, tp=tp, xw=xw):
                        gp.wait_ge(vs.h, rm[-1])
                        dX[tp].dma(gp.dma_start(xdst_dram[b, t * 128:(t + 1) * 128, :], xw[:]))
                    blk.gpsimd(_gp)

                    if ln_mode is not None:
                        mode, ln_dst = ln_mode
                        lnsem = dL if ln_dst is d_xln else dN
                        xnp = [xnp0, xnp1][tp]

                        def _vl1(vector, xw=xw, mode=mode):
                            ln_vec1(vector, xw[:], mode)
                        blk.vector(_vl1)

                        blk.scalar(ln_scalar)

                        def _vl2(vector, xw=xw, xnp=xnp):
                            ln_vec2(vector, xw[:], xnp[:])
                            rm[-1] = vs.n
                        blk.vector(_vl2)

                        def _gp2(gp, t=t, tp=tp, xnp=xnp, ln_dst=ln_dst, lnsem=lnsem):
                            gp.wait_ge(vs.h, rm[-1])
                            lnsem[tp].dma(gp.dma_start(
                                ln_dst[b, t * 128:(t + 1) * 128, :], xnp[:]))
                        blk.gpsimd(_gp2)

        def emit_attention(b, weave=None):
            def _pre(sync):
                wait_stream_totals(sync, [dL])
                if mPV.n:
                    sync.wait_ge(mPV.h, mPV.n)
            blk.sync(_pre)

            if b > 0:
                def _gw(gp, tsn=ts.n):
                    gp.wait_ge(ts.h, tsn)
                    gld.dma(gp.dma_start(w_a[:].rearrange("p (k j) -> p k j", j=512),
                                         wq_d.rearrange("(k p) j -> p k j", p=128)))
                blk.gpsimd(_gw)

            def _vm(vector, tsn=ts.n):
                vector.wait_ge(ts.h, tsn)
                vs.inc(vector.memset(v_aug[:], 1.0))
            blk.vector(_vm)

            slot_cnt = [0]
            freed = []
            sq_marks = []
            tsq_mk = {}
            gld_gate = gld.n

            def emit_proj(I):
                p = I % 2
                sa = [aA0, aA1][p]
                pm = []
                for kind in range(12):
                    s = slot_cnt[0]
                    slot_cnt[0] += 1

                    def _t(tensor, I=I, p=p, sa=sa, kind=kind, s=s, gg=gld_gate,
                           wdv=wo_done_vs[0]):
                        if s == 0 and gg:
                            tensor.wait_ge(gld.h, gg)
                        if s == 0 and wdv:
                            tensor.wait_ge(vs.h, wdv)
                        tensor.wait_ge(dB[p].h, dB[p].n)
                        fm = mk(freed, s - 6)
                        if fm:
                            tensor.wait_ge(vs.h, fm)
                        if kind < 4:
                            wsb, pr = w_a, kind
                        elif kind < 8:
                            wsb, pr = wk_sb, kind - 4
                        else:
                            wsb, pr = wv_sb, kind - 8
                        for cch in range(NCH):
                            if kind < 8:
                                mm2 = tensor.matmul(
                                    pslot(s),
                                    wsb[:, cch * 512 + pr * 128: cch * 512 + (pr + 1) * 128],
                                    sa[:, cch * 512:(cch + 1) * 512],
                                    start=(cch == 0), stop=(cch == NCH - 1))
                            else:
                                mm2 = tensor.matmul(
                                    pslot(s),
                                    sa[:, cch * 512 + pr * 128: cch * 512 + (pr + 1) * 128],
                                    wv_sb[:, cch * 512:(cch + 1) * 512],
                                    start=(cch == 0), stop=(cch == NCH - 1))
                        ts.inc(mm2)
                        pm.append(ts.n)
                    blk.tensor(_t)

                    def _v(vector, I=I, kind=kind, s=s):
                        vector.wait_ge(ts.h, pm[-1])
                        if kind < 8:
                            dst_sb = qT_sb if kind < 4 else kT_sb
                            pr = kind % 4
                            dsl = dst_sb[:, pr * 4096 + I * 512: pr * 4096 + (I + 1) * 512]
                            vs.inc(vector.tensor_copy(dsl, pslot(s)))
                            freed.append(vs.n)
                            j = len(sq_marks)
                            tm = tsq_mk.get(j - 8, 0)
                            if tm:
                                vector.wait_ge(ts.h, tm)
                            vs.inc(vector.tensor_tensor(
                                ao_sb0[:, (j % 8) * 512: (j % 8 + 1) * 512],
                                dsl, dsl, ALU.mult))
                            sq_marks.append(vs.n)
                        else:
                            jcg = I * 4 + (kind - 8)
                            dst = v_aug[:, jcg * 520:(jcg + 1) * 520].rearrange(
                                "p (h x) -> p h x", x=65)[:, :, 0:64]
                            src = pslot(s).rearrange("p (h x) -> p h x", x=64)
                            vs.inc(vector.tensor_copy(dst, src))
                            freed.append(vs.n)
                    blk.vector(_v)
                return pm

            def emit_norm(I, sq_base):
                nm = []
                for half, col0 in ((0, 1024), (1, 1536)):
                    for pr in range(4):
                        j = sq_base + half * 4 + pr

                        def _t(tensor, half=half, col0=col0, pr=pr, j=j):
                            tensor.wait_ge(vs.h, sq_marks[j])
                            mm = tensor.matmul(pB[0:8, col0:col0 + 512],
                                               bones8[:, pr * 8:(pr + 1) * 8],
                                               ao_sb0[:, (j % 8) * 512: (j % 8 + 1) * 512],
                                               start=(pr == 0), stop=(pr == 3))
                            ts.inc(mm)
                            tsq_mk[j] = ts.n
                            nm.append(ts.n)
                        blk.tensor(_t)

                def _v(vector):
                    vector.wait_ge(ts.h, nm[-1])
                    vector.tensor_scalar(nrec[0:8, 0:512], pB[0:8, 1024:1536],
                                         1e-24, None, ALU.max)
                    vs.inc(vector.tensor_scalar(nrec[0:8, 512:1024], pB[0:8, 1536:2048],
                                                1e-24, None, ALU.max))
                    vector.wait_ge(vs.h, vs.n)
                    vector.reciprocal(nrec[0:8, 0:512], nrec[0:8, 0:512])
                    v2s.inc(vector.reciprocal(nrec[0:8, 512:1024], nrec[0:8, 512:1024]))
                blk.vector(_v)

                def _sc(scalar):
                    scalar.wait_ge(v2s.h, v2s.n)
                    s2v.inc(scalar.activation(rinv_n[0:8, :], nrec[0:8, :], AF.Sqrt))
                blk.scalar(_sc)

                bm = []
                for half in range(2):
                    for pr in range(4):
                        s = slot_cnt[0]
                        slot_cnt[0] += 1

                        def _t(tensor, half=half, pr=pr, s=s):
                            tensor.wait_ge(s2v.h, s2v.n)
                            fm = mk(freed, s - 6)
                            if fm:
                                tensor.wait_ge(vs.h, fm)
                            mm = tensor.matmul(pslot(s), sel8[0:8, pr * 128:(pr + 1) * 128],
                                               rinv_n[0:8, half * 512:(half + 1) * 512],
                                               start=True, stop=True)
                            ts.inc(mm)
                            bm.append(ts.n)
                        blk.tensor(_t)

                        def _v(vector, I=I, half=half, pr=pr, s=s):
                            vector.wait_ge(ts.h, bm[-1])
                            dst_sb = qT_sb if half == 0 else kT_sb
                            dsl = dst_sb[:, pr * 4096 + I * 512: pr * 4096 + (I + 1) * 512]
                            vs.inc(vector.tensor_tensor(dsl, dsl, pslot(s), ALU.mult))
                            freed.append(vs.n)
                        blk.vector(_v)

            pm_by_stripe = []
            for I in range(N_STRIPES):
                p = I % 2
                sa = [aA0, aA1][p]

                def _sy(sync, I=I, p=p, sa=sa):
                    if I >= 2:
                        sync.wait_ge(ts.h, pm_by_stripe[I - 2][-1])
                    for cch in range(NCH):
                        dB[p].dma(sync.dma_start(
                            sa[:, cch * 512:(cch + 1) * 512],
                            d_xln[b, I * 512:(I + 1) * 512, cch * 128:(cch + 1) * 128],
                            transpose=True))
                blk.sync(_sy)

                if I >= 1:
                    emit_norm(I - 1, (I - 1) * 8)
                pm_by_stripe.append(emit_proj(I))
            emit_norm(N_STRIPES - 1, (N_STRIPES - 1) * 8)

            def _gw2(gp, tsn=ts.n):
                gp.wait_ge(ts.h, tsn)
                gld.dma(gp.dma_start(w_a[:].rearrange("p (m u) -> p m u", u=1024),
                                     wo_d.rearrange("(m p) u -> p m u", p=128)))
            blk.gpsimd(_gw2)
            wo_gate = gld.n

            prep_vs = vs.n

            phx = [aA0[:, q * 1024:(q + 1) * 1024] for q in range(4)]
            nmm = []
            pend_t4 = [None]
            pend_pv = [None]

            def emit_pv_tail(tensor, args):
                m_p, eA_tgt = args
                tensor.wait_ge(eA.h, eA_tgt)
                pj = N_JC - 1
                phb = phx[pj % 4]
                va0 = pj * 520 + 130 * m_p
                tensor.matmul(pB[0:65, 0:512],
                              v_aug[:, va0:va0 + 65],
                              phb[:, 0:512],
                              start=False, stop=True)
                mm2 = tensor.matmul(pB[0:65, 512:1024],
                                    v_aug[:, va0 + 65:va0 + 130],
                                    phb[:, 512:1024],
                                    start=False, stop=True)
                mPV.inc(mm2)

            def emit_t4(tensor, args):
                rsc_mark = args[0]
                tensor.wait_ge(vs.h, rsc_mark)
                tensor.matmul(pB[0:64, 1024:1536], ones64[0:1, :],
                              rinv_ib[0:1, 0:512],
                              start=True, stop=True, tile_position=(0, 0))
                mm2 = tensor.matmul(pB[64:128, 1024:1536], ones64[0:1, :],
                                    rinv_ib[0:1, 512:1024],
                                    start=True, stop=True, tile_position=(0, 64))
                ts.inc(mm2)
                nmm.append(ts.n)

            def emit_v5(vector, paov, pi0, nm):
                vector.wait_ge(ts.h, nm)
                vs.inc(vector.reciprocal(pB[:, 1024:1536], pB[:, 1024:1536]))
                vector.wait_ge(vs.h, vs.n)
                cols = slice(pi0, pi0 + 512)
                evD.inc(vector.tensor_tensor(paov[:, cols], paov[:, cols],
                                             pB[:, 1024:1536], ALU.mult))

            for m in range(N_PAIRS):
                rsc_marks = []
                qv0 = m * 4096
                aov = ao_view(m)
                for ib8 in range(8):
                    i0 = ib8 * 512
                    eA_base = eA.n
                    mPV_base = mPV.n
                    mS_base = mS.n
                    evD_pre = evD.n

                    def _t(tensor, m=m, i0=i0, qv0=qv0, eA_base=eA_base, evD_pre=evD_pre,
                           first=(m == 0 and ib8 == 0), pt4=pend_t4[0], ppv=pend_pv[0]):
                        if first:
                            tensor.wait_ge(vs.h, prep_vs)
                        for jc in range(N_JC):
                            if jc == 1 and ppv is not None:
                                emit_pv_tail(tensor, ppv)
                            if jc == 3 and pt4 is not None:
                                emit_t4(tensor, pt4)
                            w = eA_base + jc - 1
                            if w > 0:
                                tensor.wait_ge(eA.h, w)
                            lg = pA[:, (jc % 2) * 1024:(jc % 2 + 1) * 1024]
                            tensor.matmul(lg[:, 0:512],
                                          kT_sb[0:64, qv0 + jc * 128: qv0 + (jc + 1) * 128],
                                          qT_sb[0:64, qv0 + i0: qv0 + i0 + 512],
                                          start=True, stop=True, tile_position=(0, 0))
                            mm2 = tensor.matmul(lg[:, 512:1024],
                                                kT_sb[64:128, qv0 + jc * 128: qv0 + (jc + 1) * 128],
                                                qT_sb[64:128, qv0 + i0: qv0 + i0 + 512],
                                                start=True, stop=True, tile_position=(64, 0))
                            mS.inc(mm2)
                            if jc > 0:
                                pj = jc - 1
                                if pj == 0:
                                    tensor.wait_ge(evD.h, evD_pre)
                                    if pt4 is not None:
                                        tensor.wait_ge(vs.h, pt4[0])
                                tensor.wait_ge(eA.h, eA_base + jc)
                                phb = phx[pj % 4]
                                va0 = pj * 520 + 130 * m
                                tensor.matmul(pB[0:65, 0:512],
                                              v_aug[:, va0:va0 + 65],
                                              phb[:, 0:512],
                                              start=(pj == 0), stop=False)
                                mm2 = tensor.matmul(pB[0:65, 512:1024],
                                                    v_aug[:, va0 + 65:va0 + 130],
                                                    phb[:, 512:1024],
                                                    start=(pj == 0), stop=False)
                                mPV.inc(mm2)
                    blk.tensor(_t)
                    pend_pv[0] = (m, eA_base + N_JC)

                    if pend_t4[0] is not None:
                        _, paov, pi0 = pend_t4[0]

                        def _v5p(vector, paov=paov, pi0=pi0, nm=nmm[-1]):
                            emit_v5(vector, paov, pi0, nm)
                        blk.vector(_v5p)

                    def _sc(scalar, eA_base=eA_base, mPV_base=mPV_base, mS_base=mS_base):
                        for jc in range(N_JC):
                            scalar.wait_ge(mS.h, mS_base + jc + 1)
                            w = mPV_base + jc - 2
                            if w > 0:
                                scalar.wait_ge(mPV.h, w)
                            lg = pA[:, (jc % 2) * 1024:(jc % 2 + 1) * 1024]
                            eA.inc(scalar.activation(phx[jc % 4], lg, AF.Exp,
                                                     scale=c16[:, 0:1], bias=cm16[:, 0:1]))
                    blk.scalar(_sc)

                    def _v(vector, m=m, i0=i0, aov=aov):
                        vector.wait_ge(mPV.h, mPV.n + (1 if pend_pv[0] is not None else 0))
                        cols = slice(i0, i0 + 512)
                        vector.tensor_copy(aov[0:64, cols], pB[0:64, 0:512])
                        vector.tensor_copy(aov[64:128, cols], pB[0:64, 512:1024])
                        vector.tensor_scalar(rinv_ib[0:1, 0:512], pB[64:65, 0:512],
                                             1e-6, None, ALU.max)
                        vs.inc(vector.tensor_scalar(rinv_ib[0:1, 512:1024],
                                                    pB[64:65, 512:1024],
                                                    1e-6, None, ALU.max))
                        rsc_marks.append(vs.n)
                    blk.vector(_v)

                    pend_t4[0] = (rsc_marks[-1], aov, i0)
                    if weave:
                        weave.pop(0)()

            lrsc, laov, li0 = pend_t4[0]
            pend_t4[0] = None
            lppv = pend_pv[0]
            pend_pv[0] = None

            def _t4f(tensor):
                emit_pv_tail(tensor, lppv)
                emit_t4(tensor, (lrsc,))
            blk.tensor(_t4f)

            def _v5f(vector):
                emit_v5(vector, laov, li0, nmm[-1])
            blk.vector(_v5f)

            mmm = []
            rm = []

            def _pre2(sync):
                wait_stream_totals(sync, [dX])
            blk.sync(_pre2)

            for t in range(N_TILES):
                tp = t % 2
                buf = [xsrc0, xsrc1][tp]
                xw = [xw0, xw1][tp]
                ybank = tp * 1024

                def _sy(sync, t=t, tp=tp, buf=buf):
                    if t >= 2:
                        sync.wait_ge(vs.h, mk(rm, t - 2))
                    dA[tp].dma(sync.dma_start(buf[:], d_x1[b, t * 128:(t + 1) * 128, :]))
                blk.sync(_sy)

                def _t(tensor, t=t, ybank=ybank):
                    tensor.wait_ge(evD.h, evD.n)
                    if t == 0:
                        tensor.wait_ge(gld.h, wo_gate)
                    if t >= 2:
                        tensor.wait_ge(vs.h, mk(rm, t - 2))
                    for mmi in range(4):
                        lhs = ao_view(mmi)[:, t * 128:(t + 1) * 128]
                        tensor.matmul(pA[:, ybank:ybank + 512], lhs,
                                      w_a[:, mmi * 1024:mmi * 1024 + 512],
                                      start=(mmi == 0), stop=(mmi == 3))
                        mm2 = tensor.matmul(pA[:, ybank + 512:ybank + 1024], lhs,
                                            w_a[:, mmi * 1024 + 512:(mmi + 1) * 1024],
                                            start=(mmi == 0), stop=(mmi == 3))
                    ts.inc(mm2)
                    mmm.append(ts.n)
                blk.tensor(_t)

                def _v(vector, t=t, tp=tp, buf=buf, xw=xw, ybank=ybank):
                    vector.wait_ge(ts.h, mmm[-1])
                    vector.wait_ge(dA[tp].h, dA[tp].n)
                    vector.wait_ge(dX[tp].h, dX[tp].n)
                    if STAGE >= 3:
                        vector.wait_ge(dN[tp].h, dN[tp].n)
                    vs.inc(vector.tensor_tensor(xw[:], pA[:, ybank:ybank + 1024], buf[:], ALU.add))
                    rm.append(vs.n)
                blk.vector(_v)

                def _gp(gp, t=t, tp=tp, xw=xw):
                    gp.wait_ge(vs.h, rm[-1])
                    dX[tp].dma(gp.dma_start(
                        (d_x2 if STAGE >= 3 else out_d)[b, t * 128:(t + 1) * 128, :], xw[:]))
                blk.gpsimd(_gp)

                if STAGE >= 3:
                    xnp = [xnp0, xnp1][tp]

                    def _vl1(vector, xw=xw):
                        ln_vec1(vector, xw[:], "ffn")
                    blk.vector(_vl1)

                    blk.scalar(ln_scalar)

                    def _vl2(vector, xw=xw, xnp=xnp):
                        ln_vec2(vector, xw[:], xnp[:])
                        rm[-1] = vs.n
                    blk.vector(_vl2)

                    def _gp2(gp, t=t, tp=tp, xnp=xnp):
                        gp.wait_ge(vs.h, rm[-1])
                        dN[tp].dma(gp.dma_start(d_xnt[b, t * 128:(t + 1) * 128, :], xnp[:]))
                    blk.gpsimd(_gp2)
            wo_done_vs[0] = vs.n

        emit_init()
        init_gate()
        fence()

        for b in range(B_PER_CORE):
            emit_ffn_A(b, x_in, "ffn")
            fence()
        for b in range(B_PER_CORE):
            emit_ffn_B1(b, cmcol0=b * 8)
            fence()
        emit_gamma(cc_in1, cc_out1)
        for b in range(B_PER_CORE):
            emit_ffn_B2(b, x_in, out_d if STAGE == 1 else d_x1,
                        None if STAGE == 1 else ("plain", d_xln), res_is_x2=False)
            fence()

        def make_b1_weave(b, cmcol0):
            b1m = []
            stq = {}
            iters = []

            def mk_iter(i):
                cch, hf = i // 2, i % 2
                p = i % 2
                scr = [xw0, xw1][p][:].bitcast(BF16)

                def emit():
                    def _sy(sync, b=b, i=i, cch=cch, hf=hf, p=p, scr=scr):
                        if i == 0:
                            for s in dN + dX:
                                if s.n:
                                    sync.wait_ge(s.h, s.n)
                            sync.wait_ge(vs.h, vs.n)
                        if i >= 2:
                            sync.wait_ge(vs.h, b1m[i - 2])
                            w = stq.get(i - 2)
                            if w:
                                sync.wait_ge(dT[p].h, w)
                        dB[p].dma(sync.dma_start(
                            scr, d_xnt[b, hf * 2048:(hf + 1) * 2048,
                                       cch * 128:(cch + 1) * 128],
                            transpose=True))
                    blk.sync(_sy)

                    def _v(vector, i=i, cch=cch, hf=hf, p=p, scr=scr, cmcol0=cmcol0):
                        vector.wait_ge(dB[p].h, dB[p].n)
                        vs.inc(vector.tensor_scalar(scr, scr, g_sb[:, cch:cch + 1],
                                                    b_sb[:, cch:cch + 1],
                                                    ALU.mult, ALU.add))
                        vector.wait_ge(vs.h, vs.n)
                        dst = chmax[:, cmcol0 + cch:cmcol0 + cch + 1] if hf == 0                             else sabs[:, 0:1]
                        vs.inc(vector.tensor_reduce(dst, scr, axis=AX, op=ALU.max,
                                                    apply_absolute_value=True))
                        if hf == 1:
                            vector.wait_ge(vs.h, vs.n)
                            vs.inc(vector.tensor_tensor(
                                chmax[:, cmcol0 + cch:cmcol0 + cch + 1],
                                chmax[:, cmcol0 + cch:cmcol0 + cch + 1],
                                sabs[:, 0:1], ALU.max))
                        b1m.append(vs.n)
                    blk.vector(_v)

                    def _gp(gp, b=b, i=i, cch=cch, hf=hf, p=p, scr=scr):
                        gp.wait_ge(vs.h, b1m[-1])
                        dT[p].dma(gp.dma_start(
                            d_xnT[b, cch * 128:(cch + 1) * 128,
                                  hf * 2048:(hf + 1) * 2048], scr))
                        stq[i] = dT[p].n
                    blk.gpsimd(_gp)
                return emit

            for i in range(16):
                iters.append(mk_iter(i))
            return iters

        if STAGE >= 2:
            emit_attention(0)
            wv = make_b1_weave(0, 0) if STAGE >= 3 else None
            emit_attention(1, weave=wv)
            fence()

        if STAGE >= 3:
            for it in make_b1_weave(1, 8):
                it()
            fence()
            emit_gamma(cc_in3, cc_out3)
            for b in range(B_PER_CORE):
                emit_ffn_B2(b, d_x2, out_d, None, res_is_x2=True)
                fence()

        def _fin_g(gp):
            for s2 in (dN, dL, dX, dT):
                for s in s2:
                    if s.n:
                        gp.wait_ge(s.h, s.n)
        blk.gpsimd(_fin_g)

        def _fin_s(sync):
            for s in dX:
                if s.n:
                    sync.wait_ge(s.h, s.n)
        blk.sync(_fin_s)

    return nc


def _consts():
    b8 = np.zeros((128, 32), dtype=bf16)
    for m in range(4):
        b8[0:64, m * 8 + 2 * m] = 1
        b8[64:128, m * 8 + 2 * m + 1] = 1
    s8 = np.zeros((8, 512), dtype=bf16)
    for m in range(4):
        for c in range(128):
            s8[2 * m + c // 64, m * 128 + c] = 1
    return dict(c_bones8=b8, c_sel8=s8,
                c_ones64=np.ones((1, 64), dtype=bf16),
                c_ones1f=np.ones((1, 128), dtype=np.float32),
                c_ones128f=np.ones((128, 1), dtype=np.float32))


def kernel(L0_attn_x, L0_attn_Wq, L0_attn_Wkv, L0_attn_Wo,
           L0_ffn_weight, L0_ffn_ln_gamma, L0_ffn_ln_beta):
    x = np.ascontiguousarray(np.asarray(L0_attn_x, dtype=np.float32))
    wq = np.ascontiguousarray(np.asarray(L0_attn_Wq, dtype=np.float32))
    wkv = np.ascontiguousarray(np.asarray(L0_attn_Wkv, dtype=np.float32))
    wo = np.ascontiguousarray(np.asarray(L0_attn_Wo, dtype=np.float32))
    w = np.ascontiguousarray(np.asarray(L0_ffn_weight, dtype=np.float32))
    lg = np.ascontiguousarray(np.asarray(L0_ffn_ln_gamma, dtype=np.float32))
    lb = np.ascontiguousarray(np.asarray(L0_ffn_ln_beta, dtype=np.float32))

    consts = _consts()
    nc = build_program()
    in_maps = []
    for c in range(N_CORES):
        m = dict(x_in=x[c * B_PER_CORE:(c + 1) * B_PER_CORE],
                 wq_d=wq, wkv_d=wkv, wo_d=wo, w_d=w, lg_d=lg, lb_d=lb)
        m.update(consts)
        in_maps.append(m)
    res = run_bass_kernel_spmd(nc, in_maps, core_ids=list(range(N_CORES)))
    return np.concatenate([r["out"] for r in res.results], axis=0)


# version 26 (final)
# speedup vs baseline: 1.0229x; 1.0012x over previous
import os
import numpy as np
import ml_dtypes
import concourse.bass as bass
import concourse.mybir as mybir
from concourse.bass_utils import run_bass_kernel_spmd

F32 = mybir.dt.float32
BF16 = mybir.dt.bfloat16
AF = mybir.ActivationFunctionType
ALU = mybir.AluOpType
AX = mybir.AxisListType.X

N_CORES = 8
B_PER_CORE = 2
N_TOK = 4096
D = 1024
NCH = 8
N_TILES = 32
N_STRIPES = 8
N_PAIRS = 4
IB = 1024
N_IB = 4
N_JC = 32
QB = 128.0
CLIP_HI = float(np.float32(128.0) - np.float32(1e-5))
CLIP_LO = float(np.float32(-128.0) + np.float32(1e-5))
STAGE = int(os.environ.get("KSTAGE", "3"))

bf16 = ml_dtypes.bfloat16


class Sem:
    def __init__(self, nc, es, name):
        self.h = es.enter_context(nc.semaphore(name))
        self.n = 0

    def inc(self, instr, k=1):
        instr.then_inc(self.h, k)
        self.n += k
        return instr

    def dma(self, instr):
        instr.then_inc(self.h, 16)
        self.n += 16
        return instr


def mk(lst, j):
    return lst[j] if 0 <= j < len(lst) else 0


def build_program():
    nc = bass.Bass("TRN2", target_bir_lowering=False)

    x_in = nc.dram_tensor("x_in", [B_PER_CORE, N_TOK, D], F32, kind="ExternalInput").ap()
    wq_d = nc.dram_tensor("wq_d", [D, 512], F32, kind="ExternalInput").ap()
    wkv_d = nc.dram_tensor("wkv_d", [D, 1024], F32, kind="ExternalInput").ap()
    wo_d = nc.dram_tensor("wo_d", [512, D], F32, kind="ExternalInput").ap()
    w_d = nc.dram_tensor("w_d", [D, D], F32, kind="ExternalInput").ap()
    lg_d = nc.dram_tensor("lg_d", [D], F32, kind="ExternalInput").ap()
    lb_d = nc.dram_tensor("lb_d", [D], F32, kind="ExternalInput").ap()
    c_bones8 = nc.dram_tensor("c_bones8", [128, 32], BF16, kind="ExternalInput").ap()
    c_sel8 = nc.dram_tensor("c_sel8", [8, 512], BF16, kind="ExternalInput").ap()
    c_ones64 = nc.dram_tensor("c_ones64", [1, 64], BF16, kind="ExternalInput").ap()
    c_ones1f = nc.dram_tensor("c_ones1f", [1, 128], F32, kind="ExternalInput").ap()
    c_ones128f = nc.dram_tensor("c_ones128f", [128, 1], F32, kind="ExternalInput").ap()

    out_d = nc.dram_tensor("out", [B_PER_CORE, N_TOK, D], F32, kind="ExternalOutput").ap()

    d_xnt = nc.dram_tensor("d_xnt", [B_PER_CORE, N_TOK, D], BF16).ap()
    d_xnT = nc.dram_tensor("d_xnT", [B_PER_CORE, D, N_TOK], BF16).ap()
    d_x1 = nc.dram_tensor("d_x1", [B_PER_CORE, N_TOK, D], F32).ap()
    d_x2 = nc.dram_tensor("d_x2", [B_PER_CORE, N_TOK, D], F32).ap()
    d_xln = nc.dram_tensor("d_xln", [B_PER_CORE, N_TOK, D], BF16).ap()
    d_gred = nc.dram_tensor("d_gred", [128, 1], F32).ap()
    cc_in1 = nc.dram_tensor("cc_in1", [1, 1], F32).ap()
    cc_out1 = nc.dram_tensor("cc_out1", [1, 1], F32).ap()
    cc_in3 = nc.dram_tensor("cc_in3", [1, 1], F32).ap()
    cc_out3 = nc.dram_tensor("cc_out3", [1, 1], F32).ap()

    from contextlib import ExitStack
    es = ExitStack()
    S = lambda name, shape, dt: es.enter_context(nc.sbuf_tensor(name, shape, dt))
    with es:
        w_qs = S("w_qs", [128, 8 * 1024], BF16)
        w_a = S("w_a", [128, 4 * 1024], BF16)
        wk_sb = S("wk_sb", [128, 8 * 512], BF16)
        wv_sb = S("wv_sb", [128, 8 * 512], BF16)
        g_sb = S("g_sb", [128, 8], F32)
        b_sb = S("b_sb", [128, 8], F32)
        bones8 = S("bones8", [128, 32], BF16)
        sel8 = S("sel8", [8, 512], BF16)
        ones64 = S("ones64", [1, 64], BF16)
        ones1f = S("ones1f", [1, 128], F32)
        ones128f = S("ones128f", [128, 1], F32)
        c16 = S("c16", [128, 1], F32)
        cm16 = S("cm16", [128, 1], F32)
        sabs = S("sabs", [128, 8], F32)
        tny = S("tny", [1, 8], F32)
        t128 = S("t128", [128, 2], F32)
        qT_sb = S("qT_sb", [128, 4 * 4096], BF16)
        kT_sb = S("kT_sb", [128, 4 * 4096], BF16)
        v_aug = S("v_aug", [128, N_JC * 520], BF16)
        ao_sb0 = S("ao_sb0", [128, 4096], BF16)
        aA0 = S("aA0", [128, 4096], BF16)
        aA1 = S("aA1", [128, 4096], BF16)
        xsrc0 = S("xsrc0", [128, 1024], F32)
        xsrc1 = S("xsrc1", [128, 1024], F32)
        xw0 = S("xw0", [128, 1024], F32)
        xw1 = S("xw1", [128, 1024], F32)
        xnp0 = S("xnp0", [128, 1024], BF16)
        xnp1 = S("xnp1", [128, 1024], BF16)
        bnt = S("bnt", [128, 12], F32)
        mv = S("mv", [128, 2], F32)
        stt = S("sttt", [128, 8], F32)
        chmax = S("chmax", [128, 16], F32)
        lmax = S("lmax", [128, 1], F32)
        grow = S("grow", [1, 128], F32)
        g11 = S("g11", [1, 4], F32)
        scpack = S("scpack", [1, 2], F32)
        sqsy = S("sqsy", [128, 2], F32)
        nrec = S("nrec", [8, 1024], F32)
        rinv_n = S("rinv_n", [8, 1024], BF16)
        rs_ib = S("rs_ib", [1, 2048], F32)
        rinv_ib = S("rinv_ib", [1, 2048], BF16)

        ph0 = aA0[:, 0:2048]
        ph1 = aA0[:, 2048:4096]
        sq0 = kT_sb[:, 0:4096]
        sq1 = kT_sb[:, 4096:8192]
        tmpg = ao_sb0[:, 0:1024]
        tmpl = ao_sb0[:, 1024:2048]
        wst = xw0

        def ao_view(m):
            if m == 0:
                return ao_sb0
            return qT_sb[:, (m - 1) * 4096: m * 4096]

        pA = es.enter_context(nc.psum_tensor("pA", [128, 2048], F32))
        pB = es.enter_context(nc.psum_tensor("pB", [128, 2048], F32))

        def pslot(s):
            s = s % 6
            if s < 4:
                return pA[:, s * 512:(s + 1) * 512]
            return pB[:, (s - 4) * 512:(s - 4 + 1) * 512]

        sm = {}
        for name in ["ld", "gld", "vs", "ss", "ts", "ph",
                     "dA0", "dA1", "dB0", "dB1", "dN0", "dN1",
                     "dL0", "dL1", "dX0", "dX1", "dT0", "dT1",
                     "v2s", "s2v", "eA", "mS", "mPV", "evD",
                     "cg1", "cg2", "cg3", "cg4", "ccc"]:
            sm[name] = Sem(nc, es, name)
        ld, gld, vs, ss, ts, ph = (sm[k] for k in ["ld", "gld", "vs", "ss", "ts", "ph"])
        dA = (sm["dA0"], sm["dA1"])
        dB = (sm["dB0"], sm["dB1"])
        dN = (sm["dN0"], sm["dN1"])
        dL = (sm["dL0"], sm["dL1"])
        dX = (sm["dX0"], sm["dX1"])
        dT = (sm["dT0"], sm["dT1"])
        v2s, s2v, eA, mS, mPV, evD = (sm[k] for k in
                                      ["v2s", "s2v", "eA", "mS", "mPV", "evD"])
        cg1, cg2, cg3, cg4, ccc = (sm[k] for k in ["cg1", "cg2", "cg3", "cg4", "ccc"])

        blk = es.enter_context(nc.Block())
        ENGS = ["sync", "vector", "scalar", "tensor", "gpsimd"]
        wo_done_vs = [0]

        def fence():
            for e in ENGS:
                def _f(eng):
                    eng.nop().then_inc(ph.h, 1)
                getattr(blk, e)(_f)
            ph.n += len(ENGS)
            tgt = ph.n
            for e in ENGS:
                def _g(eng, tgt=tgt):
                    eng.wait_ge(ph.h, tgt)
                getattr(blk, e)(_g)

        def init_gate():
            vals = [(ld, ld.n), (gld, gld.n), (vs, vs.n), (ts, ts.n)]
            for e in ENGS:
                def _g(eng, vals=vals):
                    for s, v in vals:
                        if v:
                            eng.wait_ge(s.h, v)
                getattr(blk, e)(_g)

        def wait_stream_totals(eng, streams):
            for s2 in streams:
                for s in s2:
                    if s.n:
                        eng.wait_ge(s.h, s.n)

        def emit_init():
            def _sync(sync):
                sync.dma_start(bones8[:], c_bones8).then_inc(ld.h, 16)
                sync.dma_start(sel8[:], c_sel8).then_inc(ld.h, 16)
                sync.dma_start(ones64[:], c_ones64).then_inc(ld.h, 16)
                sync.dma_start(ones1f[:], c_ones1f).then_inc(ld.h, 16)
                sync.dma_start(ones128f[:], c_ones128f).then_inc(ld.h, 16)
                sync.dma_start(g_sb[:], lg_d.rearrange("(k p) -> p k", p=128),
                               allow_slow_non_contiguous=True).then_inc(ld.h, 16)
                sync.dma_start(b_sb[:], lb_d.rearrange("(k p) -> p k", p=128),
                               allow_slow_non_contiguous=True).then_inc(ld.h, 16)
                ld.n += 7 * 16
                for cch in range(8):
                    if cch >= 1:
                        sync.wait_ge(vs.h, cch)
                    sync.dma_start(wst[:], w_d[cch * 128:(cch + 1) * 128, :]).then_inc(ld.h, 16)
                    ld.n += 16
            blk.sync(_sync)

            def _gp(gp):
                gp.dma_start(w_a[:].rearrange("p (k j) -> p k j", j=512),
                             wq_d.rearrange("(k p) j -> p k j", p=128)).then_inc(gld.h, 16)
                gp.dma_start(wk_sb[:].rearrange("p (k j) -> p k j", j=512),
                             wkv_d[:, 0:512].rearrange("(k p) j -> p k j", p=128)).then_inc(gld.h, 16)
                gp.dma_start(wv_sb[:].rearrange("p (k j) -> p k j", j=512),
                             wkv_d[:, 512:1024].rearrange("(k p) j -> p k j", p=128)).then_inc(gld.h, 16)
                gld.n += 3 * 16
                gp.wait_ge(gld.h, gld.n)
            blk.gpsimd(_gp)

            def _v1(vector):
                vector.memset(c16[:], 16.0)
                vector.memset(cm16[:], -16.0)
                for cch in range(8):
                    vector.wait_ge(ld.h, (7 + cch + 1) * 16)
                    vs.inc(vector.tensor_reduce(sabs[:, cch:cch + 1], wst[:],
                                                axis=AX, op=ALU.add, apply_absolute_value=True))
            blk.vector(_v1)

            def _t1(tensor):
                tensor.wait_ge(ld.h, ld.n)
                tensor.wait_ge(vs.h, vs.n)
                ts.inc(tensor.matmul(pB[0:1, 0:8], ones128f[:, 0:1], sabs[:],
                                     start=True, stop=True))
            blk.tensor(_t1)

            def _v2(vector):
                vector.wait_ge(ts.h, ts.n)
                vs.inc(vector.tensor_reduce(tny[0:1, 0:1], pB[0:1, 0:8], axis=AX, op=ALU.add))
                vector.wait_ge(vs.h, vs.n)
                vs.inc(vector.tensor_scalar(tny[0:1, 1:2], tny[0:1, 0:1],
                                            1.0 / float(D * D), None, ALU.mult))
                vector.wait_ge(vs.h, vs.n)
                vector.tensor_scalar(tny[0:1, 2:3], tny[0:1, 1:2], 1e-5, 0.5, ALU.add, ALU.mult)
                vs.inc(vector.tensor_scalar(tny[0:1, 4:5], tny[0:1, 1:2], 0.5 / QB, None, ALU.mult))
                vector.wait_ge(vs.h, vs.n)
                vs.inc(vector.tensor_scalar(tny[0:1, 3:4], tny[0:1, 2:3], -1.0, None, ALU.mult))
            blk.vector(_v2)

            def _t2(tensor):
                tensor.wait_ge(vs.h, vs.n)
                ts.inc(tensor.matmul(pB[:, 512:514], ones1f[0:1, :], tny[0:1, 2:4],
                                     start=True, stop=True))
            blk.tensor(_t2)

            def _v3(vector):
                vector.wait_ge(ts.h, ts.n)
                vs.inc(vector.tensor_copy(t128[:], pB[:, 512:514]))
            blk.vector(_v3)

            vbase = [0]

            def _sync2(sync):
                sync.wait_ge(vs.h, vs.n)
                vbase[0] = vs.n
                for cch in range(8):
                    if cch >= 1:
                        sync.wait_ge(vs.h, vbase[0] + 2 * cch)
                    sync.dma_start(wst[:], w_d[cch * 128:(cch + 1) * 128, :]).then_inc(ld.h, 16)
                    ld.n += 16
            blk.sync(_sync2)

            def _v4(vector):
                base = ld.n - 8 * 16
                vector.wait_ge(vs.h, vs.n)
                for cch in range(8):
                    vector.wait_ge(ld.h, base + (cch + 1) * 16)
                    vector.tensor_scalar(tmpg, wst[:], t128[:, 0:1], None, ALU.is_gt)
                    vs.inc(vector.tensor_scalar(tmpl, wst[:], t128[:, 1:2], None, ALU.is_lt))
                    vector.wait_ge(vs.h, vs.n)
                    vs.inc(vector.tensor_tensor(w_qs[:, cch * 1024:(cch + 1) * 1024],
                                                tmpg, tmpl, ALU.subtract))
            blk.vector(_v4)

        def ln_vec1(vector, src_ap, mode):
            vector.wait_ge(vs.h, vs.n)
            vector.bn_stats(bnt[:, 0:6], src_ap[:, 0:512])
            vs.inc(vector.bn_stats(bnt[:, 6:12], src_ap[:, 512:1024]))
            vector.wait_ge(vs.h, vs.n)
            vs.inc(vector.bn_aggr(mv[:], bnt[:]))
            vector.wait_ge(vs.h, vs.n)
            vector.tensor_scalar(stt[:, 0:1], mv[:, 0:1], -1.0, None, ALU.mult)
            if mode == "ffn":
                vs.inc(vector.tensor_scalar(stt[:, 1:2], mv[:, 1:2], 1.001, 1e-8, ALU.mult, ALU.add))
            else:
                vs.inc(vector.tensor_scalar(stt[:, 1:2], mv[:, 1:2], 1e-5, None, ALU.add))
            vector.wait_ge(vs.h, vs.n)
            v2s.inc(vector.reciprocal(stt[:, 2:3], stt[:, 1:2]))

        def ln_scalar(scalar):
            scalar.wait_ge(v2s.h, v2s.n)
            s2v.inc(scalar.activation(stt[:, 3:4], stt[:, 2:3], AF.Sqrt))

        def ln_vec2(vector, src_ap, xnp_ap):
            vector.wait_ge(s2v.h, s2v.n)
            vs.inc(vector.tensor_scalar(xnp_ap, src_ap, stt[:, 0:1], stt[:, 3:4],
                                        ALU.add, ALU.mult))

        def emit_ffn_A(b, src_dram, mode, weave=None):
            aff = []
            for t in range(N_TILES):
                p = t % 2
                buf = [xsrc0, xsrc1][p]
                xnp = [xnp0, xnp1][p]

                def _sy(sync, t=t, p=p, buf=buf):
                    if t >= 2:
                        sync.wait_ge(vs.h, mk(aff, t - 2))
                    dA[p].dma(sync.dma_start(buf[:], src_dram[b, t * 128:(t + 1) * 128, :]))
                blk.sync(_sy)

                def _v1(vector, p=p, buf=buf):
                    vector.wait_ge(dA[p].h, dA[p].n)
                    ln_vec1(vector, buf[:], mode)
                blk.vector(_v1)

                blk.scalar(ln_scalar)

                def _v2(vector, t=t, p=p, buf=buf, xnp=xnp):
                    vector.wait_ge(dN[p].h, dN[p].n)
                    ln_vec2(vector, buf[:], xnp[:])
                    aff.append(vs.n)
                blk.vector(_v2)

                def _gp(gp, t=t, p=p, xnp=xnp):
                    gp.wait_ge(vs.h, aff[-1])
                    dN[p].dma(gp.dma_start(d_xnt[b, t * 128:(t + 1) * 128, :], xnp[:]))
                blk.gpsimd(_gp)

                if weave:
                    weave.pop(0)()

        def emit_ffn_B1(b, cmcol0):
            done = []

            def _pre(sync):
                wait_stream_totals(sync, [dN])
                if sm["mPV"].n:
                    sync.wait_ge(sm["mPV"].h, sm["mPV"].n)
            blk.sync(_pre)

            for k in range(NCH):
                p = k % 2
                buf = [aA0, aA1][p]

                def _sy(sync, k=k, p=p, buf=buf):
                    sync.wait_ge(dT[p].h, dT[p].n)
                    dB[p].dma(sync.dma_start(buf[:], d_xnt[b, :, k * 128:(k + 1) * 128],
                                             transpose=True))
                blk.sync(_sy)

                def _v(vector, k=k, p=p, buf=buf):
                    vector.wait_ge(dB[p].h, dB[p].n)
                    vs.inc(vector.tensor_scalar(buf[:], buf[:], g_sb[:, k:k + 1],
                                                b_sb[:, k:k + 1], ALU.mult, ALU.add))
                    vector.wait_ge(vs.h, vs.n)
                    vs.inc(vector.tensor_reduce(chmax[:, cmcol0 + k:cmcol0 + k + 1], buf[:],
                                                axis=AX, op=ALU.max, apply_absolute_value=True))
                    done.append(vs.n)
                blk.vector(_v)

                def _gp(gp, k=k, p=p, buf=buf):
                    gp.wait_ge(vs.h, done[-1])
                    dT[p].dma(gp.dma_start(d_xnT[b, k * 128:(k + 1) * 128, :], buf[:]))
                blk.gpsimd(_gp)

        def emit_gamma(ccin, ccout):
            def _v1(vector):
                vector.wait_ge(vs.h, vs.n)
                vs.inc(vector.tensor_reduce(lmax[:], chmax[:], axis=AX, op=ALU.max))
            blk.vector(_v1)

            def _gp1(gp):
                gp.wait_ge(vs.h, vs.n)
                cg1.dma(gp.dma_start(d_gred, lmax[:]))
            blk.gpsimd(_gp1)

            def _sy(sync):
                sync.wait_ge(cg1.h, cg1.n)
                cg2.dma(sync.dma_start(grow[:], d_gred.rearrange("p one -> one p")))
            blk.sync(_sy)

            def _v2(vector):
                vector.wait_ge(cg2.h, cg2.n)
                vs.inc(vector.tensor_reduce(g11[0:1, 0:1], grow[:], axis=AX, op=ALU.max))
            blk.vector(_v2)

            def _sc1(scalar):
                scalar.wait_ge(vs.h, vs.n)
                cg3.dma(scalar.dma_start(ccin, g11[0:1, 0:1]))
            blk.scalar(_sc1)

            def _gp2(gp):
                gp.wait_ge(cg3.h, cg3.n)
                ccc.inc(gp.collective_compute(
                    "AllReduce", ALU.max, replica_groups=[list(range(N_CORES))],
                    ins=[ccin], outs=[ccout]), 1)
            blk.gpsimd(_gp2)

            def _sc2(scalar):
                scalar.wait_ge(ccc.h, ccc.n)
                cg4.dma(scalar.dma_start(g11[0:1, 1:2], ccout))
            blk.scalar(_sc2)

            def _v3(vector):
                vector.wait_ge(cg4.h, cg4.n)
                vs.inc(vector.reciprocal(g11[0:1, 2:3], g11[0:1, 1:2]))
                vector.wait_ge(vs.h, vs.n)
                vector.tensor_scalar(scpack[0:1, 0:1], g11[0:1, 2:3], QB, None, ALU.mult)
                vs.inc(vector.tensor_scalar(scpack[0:1, 1:2], g11[0:1, 1:2], tny[0:1, 4:5],
                                            None, ALU.mult))
            blk.vector(_v3)

            def _t(tensor):
                tensor.wait_ge(vs.h, vs.n)
                ts.inc(tensor.matmul(pB[:, 1024:1026], ones1f[0:1, :], scpack[0:1, :],
                                     start=True, stop=True))
            blk.tensor(_t)

            def _v4(vector):
                vector.wait_ge(ts.h, ts.n)
                vs.inc(vector.tensor_copy(sqsy[:], pB[:, 1024:1026]))
            blk.vector(_v4)

        def emit_ffn_B2(b, xres_dram, xdst_dram, ln_mode, res_is_x2):
            qm = []
            mmm = []
            smm = []
            rm = []

            def _pre(sync):
                wait_stream_totals(sync, [dT] + ([dX] if res_is_x2 else []))
                if sm["mPV"].n:
                    sync.wait_ge(sm["mPV"].h, sm["mPV"].n)
            blk.sync(_pre)

            for I in range(N_STRIPES):
                p = I % 2
                sa = [aA0, aA1][p]
                sq = [sq0, sq1][p]

                def _sy1(sync, I=I, p=p, sa=sa):
                    if I >= 2:
                        sync.wait_ge(vs.h, mk(qm, I - 2))
                    for cch in range(NCH):
                        dB[p].dma(sync.dma_start(
                            sa[:, cch * 512:(cch + 1) * 512],
                            d_xnT[b, cch * 128:(cch + 1) * 128, I * 512:(I + 1) * 512]))
                blk.sync(_sy1)

                def _vq(vector, I=I, p=p, sa=sa, sq=sq):
                    vector.wait_ge(vs.h, vs.n)
                    vector.wait_ge(dB[p].h, dB[p].n)
                    if I >= 2:
                        vector.wait_ge(ts.h, mk(smm, I - 2))
                    vs.inc(vector.tensor_scalar(sq, sa[:], sqsy[:, 0:1], CLIP_HI,
                                                ALU.mult, ALU.min))
                    vector.wait_ge(vs.h, vs.n)
                    vs.inc(vector.tensor_scalar(sq, sq, CLIP_LO, None, ALU.max))
                    qm.append(vs.n)
                blk.vector(_vq)

                for it in range(4):
                    t = I * 4 + it
                    tp = t % 2
                    buf = [xsrc0, xsrc1][tp]
                    xw = [xw0, xw1][tp]
                    ybank = tp * 1024

                    def _sy2(sync, t=t, tp=tp, buf=buf):
                        if t >= 2:
                            sync.wait_ge(vs.h, mk(rm, t - 2))
                        dA[tp].dma(sync.dma_start(buf[:],
                                                  xres_dram[b, t * 128:(t + 1) * 128, :]))
                    blk.sync(_sy2)

                    def _t(tensor, t=t, sq=sq, it=it, ybank=ybank, ean=eA.n):
                        if t == 0 and ean:
                            tensor.wait_ge(eA.h, ean)
                        tensor.wait_ge(vs.h, qm[-1])
                        if t >= 2:
                            tensor.wait_ge(vs.h, mk(rm, t - 2))
                        for cch in range(NCH):
                            lhs = sq[:, cch * 512 + it * 128: cch * 512 + (it + 1) * 128]
                            tensor.matmul(pA[:, ybank:ybank + 512], lhs,
                                          w_qs[:, cch * 1024:cch * 1024 + 512],
                                          start=(cch == 0), stop=(cch == NCH - 1))
                            mm2 = tensor.matmul(pA[:, ybank + 512:ybank + 1024], lhs,
                                                w_qs[:, cch * 1024 + 512:(cch + 1) * 1024],
                                                start=(cch == 0), stop=(cch == NCH - 1))
                        ts.inc(mm2)
                        mmm.append(ts.n)
                        if it == 3:
                            smm.append(ts.n)
                    blk.tensor(_t)

                    def _vr(vector, t=t, tp=tp, buf=buf, xw=xw, ybank=ybank):
                        vector.wait_ge(ts.h, mmm[-1])
                        vector.wait_ge(dA[tp].h, dA[tp].n)
                        vector.wait_ge(dX[tp].h, dX[tp].n)
                        if ln_mode is not None:
                            lnsem = dL if ln_mode[1] is d_xln else dN
                            vector.wait_ge(lnsem[tp].h, lnsem[tp].n)
                        vs.inc(vector.scalar_tensor_tensor(
                            xw[:], pA[:, ybank:ybank + 1024], sqsy[:, 1:2], buf[:],
                            ALU.mult, ALU.add))
                        rm.append(vs.n)
                    blk.vector(_vr)

                    def _gp(gp, t=t, tp=tp, xw=xw):
                        gp.wait_ge(vs.h, rm[-1])
                        dX[tp].dma(gp.dma_start(xdst_dram[b, t * 128:(t + 1) * 128, :], xw[:]))
                    blk.gpsimd(_gp)

                    if ln_mode is not None:
                        mode, ln_dst = ln_mode
                        lnsem = dL if ln_dst is d_xln else dN
                        xnp = [xnp0, xnp1][tp]

                        def _vl1(vector, xw=xw, mode=mode):
                            ln_vec1(vector, xw[:], mode)
                        blk.vector(_vl1)

                        blk.scalar(ln_scalar)

                        def _vl2(vector, xw=xw, xnp=xnp):
                            ln_vec2(vector, xw[:], xnp[:])
                            rm[-1] = vs.n
                        blk.vector(_vl2)

                        def _gp2(gp, t=t, tp=tp, xnp=xnp, ln_dst=ln_dst, lnsem=lnsem):
                            gp.wait_ge(vs.h, rm[-1])
                            lnsem[tp].dma(gp.dma_start(
                                ln_dst[b, t * 128:(t + 1) * 128, :], xnp[:]))
                        blk.gpsimd(_gp2)

        def emit_attention(b, weave=None):
            def _pre(sync):
                wait_stream_totals(sync, [dL])
                if mPV.n:
                    sync.wait_ge(mPV.h, mPV.n)
            blk.sync(_pre)

            if b > 0:
                def _gw(gp, tsn=ts.n):
                    gp.wait_ge(ts.h, tsn)
                    gld.dma(gp.dma_start(w_a[:].rearrange("p (k j) -> p k j", j=512),
                                         wq_d.rearrange("(k p) j -> p k j", p=128)))
                blk.gpsimd(_gw)

            def _vm(vector, tsn=ts.n):
                vector.wait_ge(ts.h, tsn)
                vs.inc(vector.memset(v_aug[:], 1.0))
            blk.vector(_vm)

            slot_cnt = [0]
            freed = []
            sq_marks = []
            tsq_mk = {}
            gld_gate = gld.n

            def emit_proj(I):
                p = I % 2
                sa = [aA0, aA1][p]
                pm = []
                for kind in range(12):
                    s = slot_cnt[0]
                    slot_cnt[0] += 1

                    def _t(tensor, I=I, p=p, sa=sa, kind=kind, s=s, gg=gld_gate,
                           wdv=wo_done_vs[0]):
                        if s == 0 and gg:
                            tensor.wait_ge(gld.h, gg)
                        if s == 0 and wdv:
                            tensor.wait_ge(vs.h, wdv)
                        tensor.wait_ge(dB[p].h, dB[p].n)
                        fm = mk(freed, s - 6)
                        if fm:
                            tensor.wait_ge(vs.h, fm)
                        if kind < 4:
                            wsb, pr = w_a, kind
                        elif kind < 8:
                            wsb, pr = wk_sb, kind - 4
                        else:
                            wsb, pr = wv_sb, kind - 8
                        for cch in range(NCH):
                            if kind < 8:
                                mm2 = tensor.matmul(
                                    pslot(s),
                                    wsb[:, cch * 512 + pr * 128: cch * 512 + (pr + 1) * 128],
                                    sa[:, cch * 512:(cch + 1) * 512],
                                    start=(cch == 0), stop=(cch == NCH - 1))
                            else:
                                mm2 = tensor.matmul(
                                    pslot(s),
                                    sa[:, cch * 512 + pr * 128: cch * 512 + (pr + 1) * 128],
                                    wv_sb[:, cch * 512:(cch + 1) * 512],
                                    start=(cch == 0), stop=(cch == NCH - 1))
                        ts.inc(mm2)
                        pm.append(ts.n)
                    blk.tensor(_t)

                    def _v(vector, I=I, kind=kind, s=s):
                        vector.wait_ge(ts.h, pm[-1])
                        if kind < 8:
                            dst_sb = qT_sb if kind < 4 else kT_sb
                            pr = kind % 4
                            dsl = dst_sb[:, pr * 4096 + I * 512: pr * 4096 + (I + 1) * 512]
                            vs.inc(vector.tensor_copy(dsl, pslot(s)))
                            freed.append(vs.n)
                            j = len(sq_marks)
                            tm = tsq_mk.get(j - 8, 0)
                            if tm:
                                vector.wait_ge(ts.h, tm)
                            vs.inc(vector.tensor_tensor(
                                ao_sb0[:, (j % 8) * 512: (j % 8 + 1) * 512],
                                dsl, dsl, ALU.mult))
                            sq_marks.append(vs.n)
                        else:
                            jcg = I * 4 + (kind - 8)
                            dst = v_aug[:, jcg * 520:(jcg + 1) * 520].rearrange(
                                "p (h x) -> p h x", x=65)[:, :, 0:64]
                            src = pslot(s).rearrange("p (h x) -> p h x", x=64)
                            vs.inc(vector.tensor_copy(dst, src))
                            freed.append(vs.n)
                    blk.vector(_v)
                return pm

            def emit_norm(I, sq_base):
                nm = []
                for half, col0 in ((0, 1024), (1, 1536)):
                    for pr in range(4):
                        j = sq_base + half * 4 + pr

                        def _t(tensor, half=half, col0=col0, pr=pr, j=j):
                            tensor.wait_ge(vs.h, sq_marks[j])
                            mm = tensor.matmul(pB[0:8, col0:col0 + 512],
                                               bones8[:, pr * 8:(pr + 1) * 8],
                                               ao_sb0[:, (j % 8) * 512: (j % 8 + 1) * 512],
                                               start=(pr == 0), stop=(pr == 3))
                            ts.inc(mm)
                            tsq_mk[j] = ts.n
                            nm.append(ts.n)
                        blk.tensor(_t)

                def _v(vector):
                    vector.wait_ge(ts.h, nm[-1])
                    vector.tensor_scalar(nrec[0:8, 0:512], pB[0:8, 1024:1536],
                                         1e-24, None, ALU.max)
                    vs.inc(vector.tensor_scalar(nrec[0:8, 512:1024], pB[0:8, 1536:2048],
                                                1e-24, None, ALU.max))
                    vector.wait_ge(vs.h, vs.n)
                    vector.reciprocal(nrec[0:8, 0:512], nrec[0:8, 0:512])
                    v2s.inc(vector.reciprocal(nrec[0:8, 512:1024], nrec[0:8, 512:1024]))
                blk.vector(_v)

                def _sc(scalar):
                    scalar.wait_ge(v2s.h, v2s.n)
                    s2v.inc(scalar.activation(rinv_n[0:8, :], nrec[0:8, :], AF.Sqrt))
                blk.scalar(_sc)

                bm = []
                for half in range(2):
                    for pr in range(4):
                        s = slot_cnt[0]
                        slot_cnt[0] += 1

                        def _t(tensor, half=half, pr=pr, s=s):
                            tensor.wait_ge(s2v.h, s2v.n)
                            fm = mk(freed, s - 6)
                            if fm:
                                tensor.wait_ge(vs.h, fm)
                            mm = tensor.matmul(pslot(s), sel8[0:8, pr * 128:(pr + 1) * 128],
                                               rinv_n[0:8, half * 512:(half + 1) * 512],
                                               start=True, stop=True)
                            ts.inc(mm)
                            bm.append(ts.n)
                        blk.tensor(_t)

                        def _v(vector, I=I, half=half, pr=pr, s=s):
                            vector.wait_ge(ts.h, bm[-1])
                            dst_sb = qT_sb if half == 0 else kT_sb
                            dsl = dst_sb[:, pr * 4096 + I * 512: pr * 4096 + (I + 1) * 512]
                            vs.inc(vector.tensor_tensor(dsl, dsl, pslot(s), ALU.mult))
                            freed.append(vs.n)
                        blk.vector(_v)

            pm_by_stripe = []
            for I in range(N_STRIPES):
                p = I % 2
                sa = [aA0, aA1][p]

                def _sy(sync, I=I, p=p, sa=sa):
                    if I >= 2:
                        sync.wait_ge(ts.h, pm_by_stripe[I - 2][-1])
                    for cch in range(NCH):
                        dB[p].dma(sync.dma_start(
                            sa[:, cch * 512:(cch + 1) * 512],
                            d_xln[b, I * 512:(I + 1) * 512, cch * 128:(cch + 1) * 128],
                            transpose=True))
                blk.sync(_sy)

                if I >= 1:
                    emit_norm(I - 1, (I - 1) * 8)
                pm_by_stripe.append(emit_proj(I))
            emit_norm(N_STRIPES - 1, (N_STRIPES - 1) * 8)

            def _gw2(gp, tsn=ts.n):
                gp.wait_ge(ts.h, tsn)
                gld.dma(gp.dma_start(w_a[:].rearrange("p (m u) -> p m u", u=1024),
                                     wo_d.rearrange("(m p) u -> p m u", p=128)))
            blk.gpsimd(_gw2)
            wo_gate = gld.n

            prep_vs = vs.n

            phx = [aA0[:, q * 1024:(q + 1) * 1024] for q in range(4)]
            nmm = []
            pend_t4 = [None]
            pend_pv = [None]

            def emit_pv_tail(tensor, args):
                m_p, eA_tgt = args
                tensor.wait_ge(eA.h, eA_tgt)
                pj = N_JC - 1
                phb = phx[pj % 4]
                va0 = pj * 520 + 130 * m_p
                tensor.matmul(pB[0:65, 0:512],
                              v_aug[:, va0:va0 + 65],
                              phb[:, 0:512],
                              start=False, stop=True)
                mm2 = tensor.matmul(pB[0:65, 512:1024],
                                    v_aug[:, va0 + 65:va0 + 130],
                                    phb[:, 512:1024],
                                    start=False, stop=True)
                mPV.inc(mm2)

            def emit_t4(tensor, args):
                rsc_mark = args[0]
                tensor.wait_ge(vs.h, rsc_mark)
                tensor.matmul(pB[0:64, 1024:1536], ones64[0:1, :],
                              rinv_ib[0:1, 0:512],
                              start=True, stop=True, tile_position=(0, 0))
                mm2 = tensor.matmul(pB[64:128, 1024:1536], ones64[0:1, :],
                                    rinv_ib[0:1, 512:1024],
                                    start=True, stop=True, tile_position=(0, 64))
                ts.inc(mm2)
                nmm.append(ts.n)

            def emit_v5(vector, paov, pi0, nm):
                vector.wait_ge(ts.h, nm)
                vs.inc(vector.reciprocal(pB[:, 1024:1536], pB[:, 1024:1536]))
                vector.wait_ge(vs.h, vs.n)
                cols = slice(pi0, pi0 + 512)
                evD.inc(vector.tensor_tensor(paov[:, cols], paov[:, cols],
                                             pB[:, 1024:1536], ALU.mult))

            for m in range(N_PAIRS):
                rsc_marks = []
                qv0 = m * 4096
                aov = ao_view(m)
                for ib8 in range(8):
                    i0 = ib8 * 512
                    eA_base = eA.n
                    mPV_base = mPV.n
                    mS_base = mS.n
                    evD_pre = evD.n

                    def _t(tensor, m=m, i0=i0, qv0=qv0, eA_base=eA_base, evD_pre=evD_pre,
                           first=(m == 0 and ib8 == 0), pt4=pend_t4[0], ppv=pend_pv[0]):
                        if first:
                            tensor.wait_ge(vs.h, prep_vs)
                        for jc in range(N_JC):
                            if jc == 1 and ppv is not None:
                                emit_pv_tail(tensor, ppv)
                            if jc == 3 and pt4 is not None:
                                emit_t4(tensor, pt4)
                            w = eA_base + jc - 1
                            if w > 0:
                                tensor.wait_ge(eA.h, w)
                            lg = pA[:, (jc % 2) * 1024:(jc % 2 + 1) * 1024]
                            tensor.matmul(lg[:, 0:512],
                                          kT_sb[0:64, qv0 + jc * 128: qv0 + (jc + 1) * 128],
                                          qT_sb[0:64, qv0 + i0: qv0 + i0 + 512],
                                          start=True, stop=True, tile_position=(0, 0))
                            mm2 = tensor.matmul(lg[:, 512:1024],
                                                kT_sb[64:128, qv0 + jc * 128: qv0 + (jc + 1) * 128],
                                                qT_sb[64:128, qv0 + i0: qv0 + i0 + 512],
                                                start=True, stop=True, tile_position=(64, 0))
                            mS.inc(mm2)
                            if jc > 0:
                                pj = jc - 1
                                if pj == 0:
                                    tensor.wait_ge(evD.h, evD_pre)
                                    if pt4 is not None:
                                        tensor.wait_ge(vs.h, pt4[0])
                                tensor.wait_ge(eA.h, eA_base + jc)
                                phb = phx[pj % 4]
                                va0 = pj * 520 + 130 * m
                                tensor.matmul(pB[0:65, 0:512],
                                              v_aug[:, va0:va0 + 65],
                                              phb[:, 0:512],
                                              start=(pj == 0), stop=False)
                                mm2 = tensor.matmul(pB[0:65, 512:1024],
                                                    v_aug[:, va0 + 65:va0 + 130],
                                                    phb[:, 512:1024],
                                                    start=(pj == 0), stop=False)
                                mPV.inc(mm2)
                    blk.tensor(_t)
                    pend_pv[0] = (m, eA_base + N_JC)

                    if pend_t4[0] is not None:
                        _, paov, pi0 = pend_t4[0]

                        def _v5p(vector, paov=paov, pi0=pi0, nm=nmm[-1]):
                            emit_v5(vector, paov, pi0, nm)
                        blk.vector(_v5p)

                    def _sc(scalar, eA_base=eA_base, mPV_base=mPV_base, mS_base=mS_base):
                        for jc in range(N_JC):
                            scalar.wait_ge(mS.h, mS_base + jc + 1)
                            w = mPV_base + jc - 2
                            if w > 0:
                                scalar.wait_ge(mPV.h, w)
                            lg = pA[:, (jc % 2) * 1024:(jc % 2 + 1) * 1024]
                            eA.inc(scalar.activation(phx[jc % 4], lg, AF.Exp,
                                                     scale=c16[:, 0:1], bias=cm16[:, 0:1]))
                    blk.scalar(_sc)

                    def _v(vector, m=m, i0=i0, aov=aov):
                        vector.wait_ge(mPV.h, mPV.n + (1 if pend_pv[0] is not None else 0))
                        cols = slice(i0, i0 + 512)
                        vector.tensor_copy(aov[0:64, cols], pB[0:64, 0:512])
                        vector.tensor_copy(aov[64:128, cols], pB[0:64, 512:1024])
                        vector.tensor_scalar(rinv_ib[0:1, 0:512], pB[64:65, 0:512],
                                             1e-6, None, ALU.max)
                        vs.inc(vector.tensor_scalar(rinv_ib[0:1, 512:1024],
                                                    pB[64:65, 512:1024],
                                                    1e-6, None, ALU.max))
                        rsc_marks.append(vs.n)
                    blk.vector(_v)

                    pend_t4[0] = (rsc_marks[-1], aov, i0)
                    if weave:
                        weave.pop(0)()

            lrsc, laov, li0 = pend_t4[0]
            pend_t4[0] = None
            lppv = pend_pv[0]
            pend_pv[0] = None

            def _t4f(tensor):
                emit_pv_tail(tensor, lppv)
                emit_t4(tensor, (lrsc,))
            blk.tensor(_t4f)

            def _v5f(vector):
                emit_v5(vector, laov, li0, nmm[-1])
            blk.vector(_v5f)

            mmm = []
            rm = []

            def _pre2(sync):
                wait_stream_totals(sync, [dX])
            blk.sync(_pre2)

            for t in range(N_TILES):
                tp = t % 2
                buf = [xsrc0, xsrc1][tp]
                xw = [xw0, xw1][tp]
                ybank = tp * 1024

                def _sy(sync, t=t, tp=tp, buf=buf):
                    if t >= 2:
                        sync.wait_ge(vs.h, mk(rm, t - 2))
                    dA[tp].dma(sync.dma_start(buf[:], d_x1[b, t * 128:(t + 1) * 128, :]))
                blk.sync(_sy)

                def _t(tensor, t=t, ybank=ybank):
                    tensor.wait_ge(evD.h, evD.n)
                    if t == 0:
                        tensor.wait_ge(gld.h, wo_gate)
                    if t >= 2:
                        tensor.wait_ge(vs.h, mk(rm, t - 2))
                    for mmi in range(4):
                        lhs = ao_view(mmi)[:, t * 128:(t + 1) * 128]
                        tensor.matmul(pA[:, ybank:ybank + 512], lhs,
                                      w_a[:, mmi * 1024:mmi * 1024 + 512],
                                      start=(mmi == 0), stop=(mmi == 3))
                        mm2 = tensor.matmul(pA[:, ybank + 512:ybank + 1024], lhs,
                                            w_a[:, mmi * 1024 + 512:(mmi + 1) * 1024],
                                            start=(mmi == 0), stop=(mmi == 3))
                    ts.inc(mm2)
                    mmm.append(ts.n)
                blk.tensor(_t)

                def _v(vector, t=t, tp=tp, buf=buf, xw=xw, ybank=ybank):
                    vector.wait_ge(ts.h, mmm[-1])
                    vector.wait_ge(dA[tp].h, dA[tp].n)
                    vector.wait_ge(dX[tp].h, dX[tp].n)
                    if STAGE >= 3:
                        vector.wait_ge(dN[tp].h, dN[tp].n)
                    vs.inc(vector.tensor_tensor(xw[:], pA[:, ybank:ybank + 1024], buf[:], ALU.add))
                    rm.append(vs.n)
                blk.vector(_v)

                def _gp(gp, t=t, tp=tp, xw=xw):
                    gp.wait_ge(vs.h, rm[-1])
                    dX[tp].dma(gp.dma_start(
                        (d_x2 if STAGE >= 3 else out_d)[b, t * 128:(t + 1) * 128, :], xw[:]))
                blk.gpsimd(_gp)

                if STAGE >= 3:
                    xnp = [xnp0, xnp1][tp]

                    def _vl1(vector, xw=xw):
                        ln_vec1(vector, xw[:], "ffn")
                    blk.vector(_vl1)

                    blk.scalar(ln_scalar)

                    def _vl2(vector, xw=xw, xnp=xnp):
                        ln_vec2(vector, xw[:], xnp[:])
                        rm[-1] = vs.n
                    blk.vector(_vl2)

                    def _gp2(gp, t=t, tp=tp, xnp=xnp):
                        gp.wait_ge(vs.h, rm[-1])
                        dN[tp].dma(gp.dma_start(d_xnt[b, t * 128:(t + 1) * 128, :], xnp[:]))
                    blk.gpsimd(_gp2)
            wo_done_vs[0] = vs.n

        def make_b1_weave(b, cmcol0):
            b1m = []
            stq = {}
            iters = []

            def mk_iter(i):
                cch, hf = i // 2, i % 2
                p = i % 2
                scr = [xw0, xw1][p][:].bitcast(BF16)

                def emit():
                    def _sy(sync, b=b, i=i, cch=cch, hf=hf, p=p, scr=scr):
                        if i == 0:
                            for s in dN + dX:
                                if s.n:
                                    sync.wait_ge(s.h, s.n)
                            sync.wait_ge(vs.h, vs.n)
                        if i >= 2:
                            sync.wait_ge(vs.h, b1m[i - 2])
                            w = stq.get(i - 2)
                            if w:
                                sync.wait_ge(dT[p].h, w)
                        dB[p].dma(sync.dma_start(
                            scr, d_xnt[b, hf * 2048:(hf + 1) * 2048,
                                       cch * 128:(cch + 1) * 128],
                            transpose=True))
                    blk.sync(_sy)

                    def _v(vector, i=i, cch=cch, hf=hf, p=p, scr=scr, cmcol0=cmcol0):
                        vector.wait_ge(dB[p].h, dB[p].n)
                        vs.inc(vector.tensor_scalar(scr, scr, g_sb[:, cch:cch + 1],
                                                    b_sb[:, cch:cch + 1],
                                                    ALU.mult, ALU.add))
                        vector.wait_ge(vs.h, vs.n)
                        dst = chmax[:, cmcol0 + cch:cmcol0 + cch + 1] if hf == 0                             else sabs[:, 0:1]
                        vs.inc(vector.tensor_reduce(dst, scr, axis=AX, op=ALU.max,
                                                    apply_absolute_value=True))
                        if hf == 1:
                            vector.wait_ge(vs.h, vs.n)
                            vs.inc(vector.tensor_tensor(
                                chmax[:, cmcol0 + cch:cmcol0 + cch + 1],
                                chmax[:, cmcol0 + cch:cmcol0 + cch + 1],
                                sabs[:, 0:1], ALU.max))
                        b1m.append(vs.n)
                    blk.vector(_v)

                    def _gp(gp, b=b, i=i, cch=cch, hf=hf, p=p, scr=scr):
                        gp.wait_ge(vs.h, b1m[-1])
                        dT[p].dma(gp.dma_start(
                            d_xnT[b, cch * 128:(cch + 1) * 128,
                                  hf * 2048:(hf + 1) * 2048], scr))
                        stq[i] = dT[p].n
                    blk.gpsimd(_gp)
                return emit

            for i in range(16):
                iters.append(mk_iter(i))
            return iters

        emit_init()
        init_gate()
        fence()

        emit_ffn_A(0, x_in, "ffn")
        fence()
        emit_ffn_A(1, x_in, "ffn", weave=make_b1_weave(0, 0))
        fence()
        emit_ffn_B1(1, cmcol0=8)
        fence()
        emit_gamma(cc_in1, cc_out1)
        for b in range(B_PER_CORE):
            emit_ffn_B2(b, x_in, out_d if STAGE == 1 else d_x1,
                        None if STAGE == 1 else ("plain", d_xln), res_is_x2=False)
            fence()

        if STAGE >= 2:
            emit_attention(0)
            wv = make_b1_weave(0, 0) if STAGE >= 3 else None
            emit_attention(1, weave=wv)
            fence()

        if STAGE >= 3:
            for it in make_b1_weave(1, 8):
                it()
            fence()
            emit_gamma(cc_in3, cc_out3)
            for b in range(B_PER_CORE):
                emit_ffn_B2(b, d_x2, out_d, None, res_is_x2=True)
                fence()

        def _fin_g(gp):
            for s2 in (dN, dL, dX, dT):
                for s in s2:
                    if s.n:
                        gp.wait_ge(s.h, s.n)
        blk.gpsimd(_fin_g)

        def _fin_s(sync):
            for s in dX:
                if s.n:
                    sync.wait_ge(s.h, s.n)
        blk.sync(_fin_s)

    return nc


def _consts():
    b8 = np.zeros((128, 32), dtype=bf16)
    for m in range(4):
        b8[0:64, m * 8 + 2 * m] = 1
        b8[64:128, m * 8 + 2 * m + 1] = 1
    s8 = np.zeros((8, 512), dtype=bf16)
    for m in range(4):
        for c in range(128):
            s8[2 * m + c // 64, m * 128 + c] = 1
    return dict(c_bones8=b8, c_sel8=s8,
                c_ones64=np.ones((1, 64), dtype=bf16),
                c_ones1f=np.ones((1, 128), dtype=np.float32),
                c_ones128f=np.ones((128, 1), dtype=np.float32))


def kernel(L0_attn_x, L0_attn_Wq, L0_attn_Wkv, L0_attn_Wo,
           L0_ffn_weight, L0_ffn_ln_gamma, L0_ffn_ln_beta):
    x = np.ascontiguousarray(np.asarray(L0_attn_x, dtype=np.float32))
    wq = np.ascontiguousarray(np.asarray(L0_attn_Wq, dtype=np.float32))
    wkv = np.ascontiguousarray(np.asarray(L0_attn_Wkv, dtype=np.float32))
    wo = np.ascontiguousarray(np.asarray(L0_attn_Wo, dtype=np.float32))
    w = np.ascontiguousarray(np.asarray(L0_ffn_weight, dtype=np.float32))
    lg = np.ascontiguousarray(np.asarray(L0_ffn_ln_gamma, dtype=np.float32))
    lb = np.ascontiguousarray(np.asarray(L0_ffn_ln_beta, dtype=np.float32))

    consts = _consts()
    nc = build_program()
    in_maps = []
    for c in range(N_CORES):
        m = dict(x_in=x[c * B_PER_CORE:(c + 1) * B_PER_CORE],
                 wq_d=wq, wkv_d=wkv, wo_d=wo, w_d=w, lg_d=lg, lb_d=lb)
        m.update(consts)
        in_maps.append(m)
    res = run_bass_kernel_spmd(nc, in_maps, core_ids=list(range(N_CORES)))
    return np.concatenate([r["out"] for r in res.results], axis=0)
